# version 04
# baseline (speedup 1.0000x reference)
import os
import numpy as np
from contextlib import ExitStack

import concourse.bass as bass
import concourse.mybir as mybir
from concourse.bass_utils import run_bass_kernel_spmd

F32 = mybir.dt.float32
F32R = mybir.dt.float32r
BF16 = mybir.dt.bfloat16
AF = mybir.ActivationFunctionType
ALU = mybir.AluOpType
AX = mybir.AxisListType.X

D = 1024
KC = 8
TC = 4096
ST = 2048
NST = TC // ST
TAU = 512
NTAU = ST // TAU
NU = ST // 128
NE = 34
NH = 16
EPS = 1e-5
NATT = TC // TAU
NLOOP = NE // 2


class Aff:
    __slots__ = ("a", "b")

    def __init__(self, a, b=0):
        self.a, self.b = a, b


class Eng:
    _uid = [0]

    def __init__(self, name, plan, emit, eng=None, sems=None, record=False):
        self.name = name
        self.plan = plan
        self.emit = emit
        self.record = record
        self.eng = eng
        self.sems = sems
        self.cnt = 0
        self.coef = 0
        self.loop_var = None
        self.in_loop = False
        self._loop_marks = []
        self._dma_id = 0
        self._ss_id = 0

    def mark(self, key):
        k = (self.name, key)
        a = self.cnt + self.coef if self.in_loop else self.cnt
        if self.record:
            if self.in_loop:
                m = Aff(a, -1)
                self.plan[k] = m
                self._loop_marks.append(k)
            else:
                self.plan[k] = Aff(a, 0)
        else:
            p = self.plan[k]
            assert p.a == a, f"drift {k}: plan={p.a} now={a}"

    def at(self, eng_name, key, delta=0, prev_iter=False):
        return (eng_name, key, delta, prev_iter)

    def wait(self, target):
        if not self.emit:
            return
        eng_name, key, delta, prev_iter = target
        aff = self.plan[(eng_name, key)]
        a, b = aff.a + delta, aff.b
        assert b >= 0, f"unpatched milestone {(eng_name, key)}"
        if prev_iter:
            a -= b
        if b == 0 or not self.in_loop:
            assert b == 0, f"affine milestone {(eng_name, key)} used outside loop"
            if a <= 0:
                return
            self.eng.wait_ge(self.sems[eng_name], a)
        else:
            assert self.loop_var is not None
            assert a >= 0, f"negative wait base {a} for {(eng_name, key)}"
            if b not in self._rcoef:
                Eng._uid[0] += 1
                r = self.eng.alloc_register(f"rc_{self.name}_{b}_{Eng._uid[0]}")
                self.eng.reg_alu(r, self.loop_var, b, op=ALU.mult)
                self._rcoef[b] = r
            self.eng.reg_alu(self._rtmp, self._rcoef[b], a, op=ALU.add)
            self.eng.wait_ge(self.sems[eng_name], self._rtmp)

    def op(self, fn_name, *args, inc=1, **kwargs):
        if fn_name in ("dma_start", "collective_compute") or                 self.name in ("vector", "scalar", "gpsimd"):
            self._dma_id += 1
            key = f"__dma_{self.name}_{self._dma_id}"
            self.mark(key)
            self.wait(self.at(self.name, key))
        if self.emit:
            ins = getattr(self.eng, fn_name)(*args, **kwargs)
            if inc:
                ins.then_inc(self.sems[self.name], inc)
        if inc:
            if self.in_loop:
                self.coef += inc
            else:
                self.cnt += inc

    def selfsync(self):
        self._ss_id += 1
        key = f"__ss_{self.name}_{self._ss_id}"
        self.mark(key)
        self.wait(self.at(self.name, key))

    def loop(self, trip):
        outer = self

        class L:
            def __enter__(self_l):
                outer.in_loop = True
                outer.coef = 0
                outer._loop_marks = []
                outer._rcoef = {}
                if outer.emit:
                    self_l._cm = outer.eng.Fori(0, trip)
                    outer.loop_var = self_l._cm.__enter__()
                    Eng._uid[0] += 1
                    outer._rtmp = outer.eng.alloc_register(f"rt_{outer.name}_{Eng._uid[0]}")
                return outer.loop_var

            def __exit__(self_l, *exc):
                if outer.emit:
                    for r in outer._rcoef.values():
                        outer.eng.free_register(r)
                    outer.eng.free_register(outer._rtmp)
                    self_l._cm.__exit__(*exc)
                if outer.record:
                    for k in outer._loop_marks:
                        outer.plan[k].b = outer.coef
                outer.cnt += outer.coef * trip
                outer.coef = 0
                outer.in_loop = False
                outer.loop_var = None
                return False
        return L()


ENGINES = ["sync", "gpsimd", "scalar", "vector", "tensor"]


def build_program(stage="full"):
    nc = bass.Bass("TRN2", target_bir_lowering=False)

    xin = nc.dram_tensor("xin", [TC, D], F32, kind="ExternalInput").ap()
    wexp_d = nc.dram_tensor("wexp", [NE, 1536, D], F32, kind="ExternalInput").ap()
    wq_d = nc.dram_tensor("wq", [D, D], F32, kind="ExternalInput").ap()
    wk_d = nc.dram_tensor("wk", [D, D], F32, kind="ExternalInput").ap()
    wv_d = nc.dram_tensor("wv", [D, D], F32, kind="ExternalInput").ap()
    wo_d = nc.dram_tensor("wo", [D, D], F32, kind="ExternalInput").ap()
    wrt_d = nc.dram_tensor("wrt", [D, 32], F32R, kind="ExternalInput").ap()
    bias_d = nc.dram_tensor("bqkvo", [128, 4 * KC], F32, kind="ExternalInput").ap()
    ebias_d = nc.dram_tensor("ebias", [128, 32], F32, kind="ExternalInput").ap()
    idf_d = nc.dram_tensor("idf", [128, 128], F32, kind="ExternalInput").ap()
    idb_d = nc.dram_tensor("idb", [128, 128], BF16, kind="ExternalInput").ap()
    id32_d = nc.dram_tensor("id32", [32, 32], F32, kind="ExternalInput").ap()
    sel2_d = nc.dram_tensor("sel2", [2, 128], F32R, kind="ExternalInput").ap()
    bmask_d = nc.dram_tensor("bmask", [128, 2], F32, kind="ExternalInput").ap()
    cst_d = nc.dram_tensor("cst", [128, 4], F32, kind="ExternalInput").ap()

    out_d = nc.dram_tensor("out", [TC, D], F32, kind="ExternalOutput").ap()
    x2_scr = nc.dram_tensor("x2_scr", [TC, D], F32).ap()
    x3_scr = nc.dram_tensor("x3_scr", [TC, D], F32).ap()
    q_scr = nc.dram_tensor("q_scr", [D, TC], BF16).ap()
    comb_dram = nc.dram_tensor("comb_dram", [NE, ST], F32).ap()
    cc_in = nc.dram_tensor("cc_in", [128, 4 * 260], F32)
    cc_out = nc.dram_tensor("cc_out", [128, 4 * 260], F32)

    ctx = ExitStack()

    def sbt(name, shape, dtype):
        return ctx.enter_context(nc.sbuf_tensor(name, shape, dtype))

    idf_t = sbt("idf_t", [128, 128], F32)
    id32_t = sbt("id32_t", [32, 32], F32)
    idb_t = sbt("idb_t", [128, 128], BF16)
    sel2_t = sbt("sel2_t", [2, 128], F32R)
    wrt_t = sbt("wrt_t", [128, KC, 32], F32R)
    ebias_t = sbt("ebias_t", [128, 32], F32)
    bias_t = sbt("bias_t", [128, 4 * KC], F32)
    bmask_t = sbt("bmask_t", [128, 2], F32)
    cst_t = sbt("cst_t", [128, 4], F32)

    wexp_buf = [sbt(f"wexp{i}", [128, 12, D], BF16) for i in range(2)]

    xn_fm = sbt("xn_fm", [128, KC, ST], BF16)
    dacc = sbt("dacc", [128, KC, ST], F32)
    comb = sbt("comb", [NE, ST], F32)
    wbc_t = sbt("wbc", [128, ST], F32)
    wbc = [wbc_t, wbc_t]
    x_tm = [sbt(f"x_tm{i}", [128, D], F32) for i in range(2)]
    xr_tm = x_tm
    o_tm = [sbt(f"o_tm{i}", [128, D], F32) for i in range(2)]
    xn_tm = sbt("xn_tm", [128, D], F32)
    xn32 = sbt("xn32", [128, KC, 128], F32R)
    stat = sbt("stat", [128, 16], F32)
    lg_s = sbt("lg_s", [32, 128], F32)
    lg_tm = sbt("lg_tm", [128, 32], F32)
    ex_t = sbt("ex_t", [128, 32], F32)
    rw_t = sbt("rw_t", [128, 32], F32)
    corr_t = sbt("corr_t", [128, 32], F32)
    m8_t = sbt("m8_t", [128, 8], F32)
    mask_t = sbt("mask_t", [128, 32], F32)
    comb_tm = sbt("comb_tm", [128, 32], F32)

    sg32 = [sbt(f"sg32_{i}", [128, TAU], F32) for i in range(2)]
    uw_t = [sbt(f"uw_{i}", [128, TAU], F32) for i in range(2)]
    hmid = sbt("hmid", [128, 4, TAU], BF16)

    def carve(base, byte_off, shape, dtype):
        tot = 1
        for d in base.shape[1:]:
            tot *= d
        h = base.reshape([128, tot]).bitcast(dtype)
        esz = {F32: 4, F32R: 4, BF16: 2}[dtype]
        n = 1
        for d in shape[1:]:
            n *= d
        assert byte_off % esz == 0
        e0 = byte_off // esz
        ap = h[:, e0:e0 + n]
        if len(shape) == 3:
            ap = ap.rearrange("p (a b) -> p a b", b=shape[2])
        return ap

    of_t = carve(dacc, 0, [128, KC, TAU], F32)
    asc_t = carve(dacc, 16384, [128, KC, TAU], BF16)
    qf_t = carve(dacc, 24576, [128, KC, TAU], BF16)
    kb_t = carve(dacc, 32768, [128, KC, TAU], BF16)
    vb_t = carve(dacc, 40960, [128, KC, TAU], BF16)
    xn_a = carve(dacc, 49152, [128, KC, TAU], BF16)
    cc_sb = carve(dacc, 57344, [128, 4 * 260], F32)
    z_ch = [carve(xn_fm, 0 + i * 2048, [128, TAU], F32) for i in range(2)]
    r_ch = [carve(xn_fm, 4096 + i * 2048, [128, TAU], F32) for i in range(2)]
    mn_ch = [carve(xn_fm, 8192 + i * 2048, [128, TAU], F32) for i in range(2)]

    kv_sb = carve(xn_fm, 20480, [128, 2 * 260], F32)
    kv_my = carve(xn_fm, 22560, [128, 2 * 260], F32)
    k_tm = carve(xn_fm, 24640, [128, D], BF16)
    v_aug = carve(xn_fm, 26688, [128, NH * 65], BF16)
    kv_bf = carve(xn_fm, 28768, [128, 2, 260], BF16)
    kvz = carve(dacc, 61504, [128, KC, 128], BF16)
    den_l = carve(xn_fm, 29808, [128, 16], BF16)
    zb_t = [carve(xn_fm, 12288 + i * 2048, [128, TAU], F32) for i in range(2)]
    den_sb = carve(xn_fm, 16384, [128, TAU], F32)[0:2, :]
    zc_full = sbt("zc_t", [2, TAU], F32R)
    zc_t = zc_full[0:2, :]
    rc32 = carve(xn_fm, 29952, [128, TAU], F32)[0:2, :]

    pctx = ExitStack()
    pbl = [pctx.enter_context(nc.psum_tensor(f"pbl{i}", [128, 1024], F32)) for i in range(3)]
    pbf = pctx.enter_context(nc.psum_tensor("pbf", [128, 2048], BF16))

    sem_ctx = ExitStack()
    sems = {n: sem_ctx.enter_context(nc.semaphore(f"s_{n}")) for n in ENGINES}

    plan = {}

    def bank(i, p0=0, np_=128, c0=0, nc_=512):
        t = pbl[i // 2]
        off = (i % 2) * 512
        return t[p0:p0 + np_, off + c0: off + c0 + nc_]

    def bankpair(j):
        return pbl[j][:, :]

    def barrier(G, name, waiters=("tensor", "vector", "scalar", "sync")):
        markers = ENGINES
        for m in markers:
            G[m].mark(f"bar_{name}")
        for w in waiters:
            for m in markers:
                if m != w:
                    G[w].wait(G[w].at(m, f"bar_{name}"))

    def emit_ln(G, tag, x_src, ug, dst_fm, dst_off, want_router, st_off=0, force_j=None, cut=99):
        sy, gp, sc, ve, te = (G[n] for n in ENGINES)
        xb = x_tm[ug % 2]

        if ug >= 2:
            sy.wait(sy.at("vector", f"{tag}_xn_{ug-2}"))
            sy.wait(sy.at("scalar", f"{tag}_sq_{ug-2}"))
        sy.op("dma_start", inc=16, out=xb[:], in_=x_src[ug * 128:(ug + 1) * 128, :])
        sy.mark(f"{tag}_x_{ug}")

        ve.wait(ve.at("sync", f"{tag}_x_{ug}"))
        ve.op("reduce_sum", stat[:, 0:1], xb[:], axis=AX, negate=True)
        sc.wait(sc.at("sync", f"{tag}_x_{ug}"))
        if ug >= 1:
            sc.wait(sc.at("vector", f"{tag}_var_{ug-1}"))
            sc.wait(sc.at("tensor", f"{tag}_tr_{ug-1}"))
        sc.op("activation", xn_tm[:], xb[:], AF.Square)
        sc.mark(f"{tag}_sq_{ug}")
        ve.wait(ve.at("scalar", f"{tag}_sq_{ug}"))
        ve.op("reduce_sum", stat[:, 1:2], xn_tm[:], axis=AX)
        ve.selfsync()
        ve.op("tensor_scalar", stat[:, 2:3], stat[:, 0:1], 1.0 / D, None, ALU.mult)
        ve.mark(f"{tag}_s1_{ug}")
        ve.wait(ve.at("vector", f"{tag}_s1_{ug}"))
        ve.op("tensor_mul", stat[:, 10:11], stat[:, 2:3], stat[:, 2:3])
        ve.mark(f"{tag}_msq_{ug}")
        ve.wait(ve.at("vector", f"{tag}_msq_{ug}"))
        ve.op("scalar_tensor_tensor", stat[:, 3:4], stat[:, 1:2], 1.0 / D, stat[:, 10:11],
              ALU.mult, ALU.subtract)
        ve.mark(f"{tag}_var_{ug}")
        sc.wait(sc.at("vector", f"{tag}_var_{ug}"))
        sc.op("activation", stat[:, 11:12], stat[:, 3:4], AF.Ln, bias=cst_t[:, 0:1])
        sc.selfsync()
        sc.op("activation", stat[:, 4:5], stat[:, 11:12], AF.Exp, scale=cst_t[:, 2:3])
        sc.mark(f"{tag}_rstd_{ug}")
        ve.wait(ve.at("scalar", f"{tag}_rstd_{ug}"))
        ve.op("tensor_scalar", xn_tm[:], xb[:], stat[:, 2:3], stat[:, 4:5], ALU.add, ALU.mult)
        ve.mark(f"{tag}_xn_{ug}")

        j = (ug % 2) if force_j is None else force_j
        back = 2 if force_j is None else 1
        te.wait(te.at("vector", f"{tag}_xn_{ug}"))
        if ug >= back:
            te.wait(te.at("scalar", f"{tag}_evac_{ug-back}"))
            if want_router:
                te.wait(te.at("vector", f"{tag}_evac32_{ug-back}"))
        for c in range(KC):
            te.op("transpose", bankpair(j)[:, c * 128:(c + 1) * 128],
                  xn_tm[:, c * 128:(c + 1) * 128], idf_t[:],
                  inc=(1 if c == KC - 1 else 0))
        te.mark(f"{tag}_tr_{ug}")

        sc.wait(sc.at("tensor", f"{tag}_tr_{ug}"))
        sc.op("activation",
              dst_fm[:, :, dst_off:dst_off + 128],
              bankpair(j).rearrange("p (c t) -> p c t", t=128), AF.Copy)
        sc.mark(f"{tag}_evac_{ug}")
        if not want_router:
            return

        ve.wait(ve.at("tensor", f"{tag}_tr_{ug}"))
        ve.wait(ve.at("scalar", f"{tag}_evac_{ug}"))
        ve.op("tensor_copy", xn32[:].rearrange("p c t -> p (c t)"), bankpair(j))
        ve.mark(f"{tag}_evac32_{ug}")
        if cut <= 0:
            return

        te.wait(te.at("vector", f"{tag}_evac32_{ug}"))
        if ug >= 1:
            te.wait(te.at("vector", f"{tag}_lgs_{ug-1}"))
        for c in range(KC):
            te.op("matmul", bank(4, 0, 32, 0, 128), wrt_t[:, c, :], xn32[:, c, :],
                  inc=(1 if c == KC - 1 else 0), start=(c == 0), stop=(c == KC - 1))
        te.mark(f"{tag}_rmm_{ug}")
        ve.wait(ve.at("tensor", f"{tag}_rmm_{ug}"))
        ve.op("tensor_copy", lg_s[:], bank(4, 0, 32, 0, 128))
        ve.mark(f"{tag}_lgs_{ug}")
        if cut <= 1:
            return
        te.wait(te.at("vector", f"{tag}_lgs_{ug}"))
        if ug >= 1:
            te.wait(te.at("vector", f"{tag}_combem_{ug-1}"))
        te.op("transpose", bank(5, 0, 128, 0, 32), lg_s[:], id32_t[:], inc=1)
        te.mark(f"{tag}_lgtr_{ug}")
        ve.wait(ve.at("tensor", f"{tag}_lgtr_{ug}"))
        ve.op("tensor_copy", lg_tm[:], bank(5, 0, 128, 0, 32))
        if cut <= 2:
            return
        ve.op("reduce_max", stat[:, 5:6], lg_tm[:], axis=AX, negate=True)
        ve.mark(f"{tag}_nmax_{ug}")
        sc.wait(sc.at("vector", f"{tag}_nmax_{ug}"))
        sc.op("activation", ex_t[:], lg_tm[:], AF.Exp, bias=stat[:, 5:6])
        sc.mark(f"{tag}_exp_{ug}")
        ve.wait(ve.at("scalar", f"{tag}_exp_{ug}"))
        ve.op("reduce_sum", stat[:, 6:7], ex_t[:], axis=AX)
        ve.mark(f"{tag}_rsum_{ug}")
        ve.wait(ve.at("vector", f"{tag}_rsum_{ug}"))
        ve.op("reciprocal", stat[:, 7:8], stat[:, 6:7])
        ve.mark(f"{tag}_rinv_{ug}")
        if cut <= 3:
            return
        ve.wait(ve.at("vector", f"{tag}_rinv_{ug}"))
        ve.op("tensor_scalar", rw_t[:], ex_t[:], stat[:, 7:8], None, ALU.mult)
        ve.op("tensor_add", corr_t[:], rw_t[:], ebias_t[:])
        ve.op("max", out=m8_t[:], in_=corr_t[:])
        ve.mark(f"{tag}_m8_{ug}")
        if cut <= 4:
            return
        ve.wait(ve.at("vector", f"{tag}_m8_{ug}"))
        ve.op("tensor_scalar", mask_t[:], corr_t[:], m8_t[:, 5:6], None, ALU.is_ge)
        ve.op("tensor_mul", rw_t[:], rw_t[:], mask_t[:])
        ve.op("reduce_sum", stat[:, 8:9], rw_t[:], axis=AX)
        ve.mark(f"{tag}_wsum_{ug}")
        ve.wait(ve.at("vector", f"{tag}_wsum_{ug}"))
        ve.op("tensor_scalar", stat[:, 12:13], stat[:, 8:9], 1e-12, None, ALU.max)
        ve.mark(f"{tag}_wclip_{ug}")
        ve.wait(ve.at("vector", f"{tag}_wclip_{ug}"))
        ve.op("reciprocal", stat[:, 9:10], stat[:, 12:13])
        ve.mark(f"{tag}_winv_{ug}")
        ve.wait(ve.at("vector", f"{tag}_winv_{ug}"))
        ve.op("tensor_scalar", comb_tm[:], rw_t[:], stat[:, 9:10], 0.5, ALU.mult, ALU.mult)
        ve.mark(f"{tag}_comb_{ug}")
        te.wait(te.at("vector", f"{tag}_comb_{ug}"))
        te.op("transpose", bank(5, 0, 32, 0, 128), comb_tm[:], idf_t[:], inc=1)
        te.mark(f"{tag}_combtr_{ug}")
        ve.wait(ve.at("tensor", f"{tag}_combtr_{ug}"))
        ve.op("tensor_copy", comb[0:32, st_off + dst_off: st_off + dst_off + 128],
              bank(5, 0, 32, 0, 128))
        ve.mark(f"{tag}_combem_{ug}")

    def emit_experts(G, tag, nloop=NLOOP):
        sy, gp, sc, ve, te = (G[n] for n in ENGINES)

        ve.op("memset", dacc[:].rearrange("p c t -> p (c t)"), 0.0)
        ve.mark(f"{tag}_dacc0")
        te.wait(te.at("vector", f"{tag}_dacc0"))

        def prev_ref(par, t_i, idx, n_idx):
            flat = (par * NTAU + t_i) * n_idx + idx
            return flat

        def expert_body(par):
            for t_i in range(NTAU):
                for p in range(4):
                    bg, bu = 2 * (p % 2), 2 * (p % 2) + 1
                    pp, ppar, pti = p - 2, par, t_i
                    if pp < 0:
                        pp += 4
                        pti -= 1
                        if pti < 0:
                            pti += NTAU
                            ppar = 1 - ppar
                    prev = (par == 0 and t_i == 0 and p < 2)
                    te.wait(te.at("scalar", f"{tag}_silu_{ppar}_{pti}_{pp}", prev_iter=prev))
                    te.wait(te.at("vector", f"{tag}_uw_{ppar}_{pti}_{pp}", prev_iter=prev))
                    for k in range(KC):
                        te.op("matmul", bank(bg), wexp_buf[par][:, k, p * 128:(p + 1) * 128],
                              xn_fm[:, k, t_i * TAU:(t_i + 1) * TAU],
                              inc=0, start=(k == 0), stop=(k == KC - 1))
                    for k in range(KC):
                        te.op("matmul", bank(bu),
                              wexp_buf[par][:, k, 512 + p * 128: 512 + (p + 1) * 128],
                              xn_fm[:, k, t_i * TAU:(t_i + 1) * TAU],
                              inc=(1 if k == KC - 1 else 0), start=(k == 0), stop=(k == KC - 1))
                    te.mark(f"{tag}_pair_{par}_{t_i}_{p}")
                    sc.wait(sc.at("tensor", f"{tag}_pair_{par}_{t_i}_{p}"))
                    sc.wait(sc.at("vector", f"{tag}_hm_{ppar}_{pti}_{pp}", prev_iter=prev))
                    sc.op("activation", sg32[p % 2][:], bank(bg), AF.Silu)
                    sc.mark(f"{tag}_silu_{par}_{t_i}_{p}")
                    ve.wait(ve.at("tensor", f"{tag}_pair_{par}_{t_i}_{p}"))
                    ve.wait(ve.at("sync", f"{tag}_wbc_{par}"))
                    ve.op("tensor_mul", uw_t[p % 2][:], bank(bu), wbc[par][:, t_i * TAU:(t_i + 1) * TAU])
                    ve.mark(f"{tag}_uw_{par}_{t_i}_{p}")
                    ve.wait(ve.at("scalar", f"{tag}_silu_{par}_{t_i}_{p}"))
                    ve.op("tensor_mul", hmid[:, p, :], sg32[p % 2][:], uw_t[p % 2][:])
                    ve.mark(f"{tag}_hm_{par}_{t_i}_{p}")
                for h in range(KC):
                    bd = 4 + (h % 2)
                    hh, hpar, hti = h - 2, par, t_i
                    if hh < 0:
                        hh += KC
                        hti -= 1
                        if hti < 0:
                            hti += NTAU
                            hpar = 1 - hpar
                    prev = (par == 0 and t_i == 0 and h < 2)
                    te.wait(te.at("vector", f"{tag}_dn_{hpar}_{hti}_{hh}", prev_iter=prev))
                    for k in range(4):
                        if h == 0:
                            te.wait(te.at("vector", f"{tag}_hm_{par}_{t_i}_{k}"))
                        te.op("matmul", bank(bd), wexp_buf[par][:, 8 + k, h * 128:(h + 1) * 128],
                              hmid[:, k, :],
                              inc=(1 if k == 3 else 0), start=(k == 0), stop=(k == 3))
                    te.mark(f"{tag}_dnmm_{par}_{t_i}_{h}")
                    ve.wait(ve.at("tensor", f"{tag}_dnmm_{par}_{t_i}_{h}"))
                    ve.op("scalar_tensor_tensor",
                          dacc[:, h, t_i * TAU:(t_i + 1) * TAU],
                          bank(bd), 1.0, dacc[:, h, t_i * TAU:(t_i + 1) * TAU],
                          ALU.mult, ALU.add)
                    ve.mark(f"{tag}_dn_{par}_{t_i}_{h}")

        sy.op("dma_start", inc=16, out=comb_dram, in_=comb[:])
        sy.mark(f"{tag}_combd")
        with sy.loop(nloop), gp.loop(nloop), sc.loop(nloop), ve.loop(nloop), te.loop(nloop):
            for par in range(2):
                gp.wait(gp.at("tensor", f"{tag}_dnmm_{par}_3_7", prev_iter=True))
                if gp.emit:
                    e_expr = gp.loop_var * 2 + par
                    gp.op("dma_start", inc=16,
                          out=wexp_buf[par][:],
                          in_=wexp_d[e_expr].rearrange("(c p) m -> p c m", p=128))
                else:
                    gp.op("dma_start", inc=16)
                gp.mark(f"{tag}_wld_{par}")
            for par in range(2):
                ppar_w = 1 - par
                prev_w = (par == 0)
                sy.wait(sy.at("vector", f"{tag}_uw_{ppar_w}_3_3", prev_iter=prev_w))
                if sy.emit:
                    e_expr = sy.loop_var * 2 + par
                    sy.op("dma_start", inc=16, out=wbc[par][:],
                          in_=comb_dram[bass.ts(e_expr, 1), :].broadcast_to([128, ST]))
                else:
                    sy.op("dma_start", inc=16)
                sy.mark(f"{tag}_wbc_{par}")
            for par in range(2):
                te.wait(te.at("gpsimd", f"{tag}_wld_{par}"))
                expert_body(par)
        for e in ENGINES:
            G[e].mark(f"{tag}_sweep_end")

    def emit_resid(G, tag, x_src, dst, st):
        sy, gp, sc, ve, te = (G[n] for n in ENGINES)
        for u in range(NU):
            ug = st * NU + u
            j = u % 2
            if u == 0:
                te.wait(te.at("vector", f"{tag}_sweep_end"))
                te.wait(te.at("scalar", f"{tag}_sweep_end"))
                te.wait(te.at("gpsimd", f"{tag}_sweep_end"))
            if u >= 2:
                te.wait(te.at("vector", f"{tag}_radd_{u-2}"))
            for c in range(KC):
                te.op("transpose", bankpair(j)[:, c * 128:(c + 1) * 128],
                      dacc[:, c, u * 128:(u + 1) * 128], idf_t[:],
                      inc=(1 if c == KC - 1 else 0))
            te.mark(f"{tag}_rtr_{u}")
            if u >= 2:
                sc.wait(sc.at("vector", f"{tag}_radd_{u-2}"))
            sc.op("dma_start", inc=16, out=xr_tm[j][:], in_=x_src[ug * 128:(ug + 1) * 128, :])
            sc.mark(f"{tag}_rx_{u}")
            ve.wait(ve.at("tensor", f"{tag}_rtr_{u}"))
            ve.wait(ve.at("scalar", f"{tag}_rx_{u}"))
            if u >= 2:
                ve.wait(ve.at("sync", f"{tag}_rst_{u-2}"))
            ve.op("tensor_add", o_tm[j][:], bankpair(j), xr_tm[j][:])
            ve.mark(f"{tag}_radd_{u}")
            sy.wait(sy.at("vector", f"{tag}_radd_{u}"))
            sy.op("dma_start", inc=16, out=dst[ug * 128:(ug + 1) * 128, :], in_=o_tm[j][:])
            sy.mark(f"{tag}_rst_{u}")

    def emit_f_call(G, cid, x_src, dst):
        for st in range(NST):
            tag = f"f{cid}s{st}"
            barrier(G, f"{tag}_a")
            for u in range(NU):
                emit_ln(G, f"f{cid}ln", x_src, st * NU + u, xn_fm, u * 128, True)
            barrier(G, f"{tag}_b")
            emit_experts(G, tag)
            barrier(G, f"{tag}_c")
            emit_resid(G, tag, x_src, dst, st)

    def emit_attn(G, src, dst):
        sy, gp, sc, ve, te = (G[n] for n in ENGINES)
        barrier(G, "att_w")
        gp.wait(gp.at("tensor", "bar_att_w"))
        gp.op("dma_start", inc=16, out=wexp_buf[0][:, 0:KC, :],
              in_=wq_d.rearrange("(c p) m -> p c m", p=128))
        gp.op("dma_start", inc=16, out=wexp_buf[1][:, 0:KC, :],
              in_=wk_d.rearrange("(c p) m -> p c m", p=128))
        gp.op("dma_start", inc=16, out=wexp_buf[0][:, 8:12, :],
              in_=wv_d[0:512].rearrange("(c p) m -> p c m", p=128))
        gp.op("dma_start", inc=16, out=wexp_buf[1][:, 8:12, :],
              in_=wv_d[512:1024].rearrange("(c p) m -> p c m", p=128))
        gp.mark("att_wld")
        te.wait(te.at("gpsimd", "att_wld"))
        ve.op("memset", v_aug[:], 1.0)
        ve.mark("att_vaug")

        for t_i in range(NATT):
            tg = f"at{t_i}"
            for s in range(4):
                emit_ln(G, "aln", src, t_i * 4 + s, xn_a, s * 128, False, force_j=0)
            for w_i in range(3):
                for c in range(KC):
                    bq = 2 + (c % 2)
                    if w_i == 0 and c == 0:
                        te.wait(te.at("scalar", f"aln_evac_{t_i*4+3}"))
                    if w_i == 0 and c < 2:
                        if t_i > 0:
                            te.wait(te.at("scalar", f"at{t_i-1}_zev_2_{KC-2+c}"))
                    elif c < 2:
                        te.wait(te.at("scalar", f"{tg}_zev_{w_i-1}_{KC-2+c}"))
                    else:
                        te.wait(te.at("scalar", f"{tg}_zev_{w_i}_{c-2}"))
                    for k in range(KC):
                        if w_i < 2:
                            lhs = wexp_buf[w_i][:, k, c * 128:(c + 1) * 128]
                        else:
                            lhs = wexp_buf[k // 4][:, 8 + k % 4, c * 128:(c + 1) * 128]
                        te.op("matmul", bank(bq), lhs, xn_a[:, k, :],
                              inc=(1 if k == KC - 1 else 0), start=(k == 0), stop=(k == KC - 1))
                    te.mark(f"{tg}_mm_{w_i}_{c}")
                    sc.wait(sc.at("tensor", f"{tg}_mm_{w_i}_{c}"))
                    if w_i < 2:
                        if c >= 2:
                            sc.wait(sc.at("gpsimd", f"{tg}_fma_{w_i}_{c-2}"))
                        elif w_i == 1:
                            sc.wait(sc.at("gpsimd", f"{tg}_fma_0_{KC-2+c}"))
                        elif t_i > 0:
                            sc.wait(sc.at("gpsimd", f"at{t_i-1}_fma_1_{KC-2+c}"))
                        sc.op("activation", z_ch[c % 2][:], bank(bq), AF.Identity,
                              bias=bias_t[:, w_i * KC + c: w_i * KC + c + 1])
                    else:
                        if c < 2 and t_i > 0:
                            sc.wait(sc.at("tensor", f"at{t_i-1}_vtr_3"))
                        sc.op("activation", vb_t[:, c, :], bank(bq), AF.Identity,
                              bias=bias_t[:, 2 * KC + c: 2 * KC + c + 1])
                    sc.mark(f"{tg}_zev_{w_i}_{c}")
                    if w_i < 2:
                        gp.wait(gp.at("scalar", f"{tg}_zev_{w_i}_{c}"))
                        if w_i == 0 and c < 2 and t_i > 0:
                            gp.wait(gp.at("sync", f"at{t_i-1}_qst"))
                            gp.wait(gp.at("tensor", f"at{t_i-1}_ktr_3"))
                        gp.op("tensor_scalar", r_ch[c % 2][:], z_ch[c % 2][:], 0.0, None, ALU.max)
                        gp.op("tensor_sub", mn_ch[c % 2][:], z_ch[c % 2][:], r_ch[c % 2][:])
                        gp.mark(f"{tg}_fma_{w_i}_{c}")
                        sc.wait(sc.at("gpsimd", f"{tg}_fma_{w_i}_{c}"))
                        if c >= 2:
                            sc.wait(sc.at("vector", f"{tg}_fmc_{w_i}_{c-2}"))
                        elif w_i == 1:
                            sc.wait(sc.at("vector", f"{tg}_fmc_0_{KC-2+c}"))
                        elif t_i > 0:
                            sc.wait(sc.at("vector", f"at{t_i-1}_fmc_1_{KC-2+c}"))
                        sc.op("activation", mn_ch[c % 2][:], mn_ch[c % 2][:], AF.Exp)
                        sc.mark(f"{tg}_fmb_{w_i}_{c}")
                        ve.wait(ve.at("scalar", f"{tg}_fmb_{w_i}_{c}"))
                        dstt = qf_t if w_i == 0 else kb_t
                        ve.op("tensor_add", dstt[:, c, :], mn_ch[c % 2][:], r_ch[c % 2][:])
                        ve.mark(f"{tg}_fmc_{w_i}_{c}")
            sy.wait(sy.at("vector", f"{tg}_fmc_0_{KC-1}"))
            sy.op("dma_start", inc=16,
                  out=q_scr.rearrange("(c p) t -> p c t", p=128)[:, :, t_i * TAU:(t_i + 1) * TAU],
                  in_=qf_t[:])
            sy.mark(f"{tg}_qst")
            for s in range(4):
                te.wait(te.at("vector", f"{tg}_fmc_1_{KC-1}"))
                if not (t_i == 0 and s == 0):
                    te.wait(te.at("vector", f"{tg}_kev_{s-1}" if s > 0 else f"at{t_i-1}_kev_3"))
                for c in range(KC):
                    te.op("transpose", pbf[:, c * 128:(c + 1) * 128],
                          kb_t[:, c, s * 128:(s + 1) * 128], idb_t[:],
                          inc=(1 if c == KC - 1 else 0))
                te.mark(f"{tg}_ktr_{s}")
                ve.wait(ve.at("tensor", f"{tg}_ktr_{s}"))
                if not (t_i == 0 and s == 0):
                    ve.wait(ve.at("tensor", f"{tg}_kv_{s-1}" if s > 0 else f"at{t_i-1}_kv_3"))
                ve.op("tensor_copy", k_tm[:], pbf[:, 0:1024])
                ve.mark(f"{tg}_kev_{s}")
                te.wait(te.at("scalar", f"{tg}_zev_2_{KC-1}"))
                if not (t_i == 0 and s == 0):
                    te.wait(te.at("vector", f"{tg}_vev_{s-1}" if s > 0 else f"at{t_i-1}_vev_3"))
                for c in range(KC):
                    te.op("transpose", pbf[:, 1024 + c * 128: 1024 + (c + 1) * 128],
                          vb_t[:, c, s * 128:(s + 1) * 128], idb_t[:],
                          inc=(1 if c == KC - 1 else 0))
                te.mark(f"{tg}_vtr_{s}")
                ve.wait(ve.at("tensor", f"{tg}_vtr_{s}"))
                ve.op("tensor_copy",
                      v_aug[:].rearrange("p (h n) -> p h n", n=65)[:, :, 0:64],
                      pbf[:, 1024:2048].rearrange("p (h n) -> p h n", n=64))
                ve.mark(f"{tg}_vev_{s}")
                te.wait(te.at("vector", f"{tg}_kev_{s}"))
                te.wait(te.at("vector", f"{tg}_vev_{s}"))
                first = (t_i == 0 and s == 0)
                last = (t_i == NATT - 1 and s == 3)
                for h in range(NH):
                    b_i = 4 + h // 8
                    g_i = (h % 8) // 2
                    pr = (h % 2) * 64
                    te.op("matmul", bank(b_i, pr, 64, g_i * 65, 65),
                          k_tm[:, h * 64:(h + 1) * 64],
                          v_aug[:, h * 65:(h + 1) * 65],
                          inc=(1 if h == NH - 1 else 0),
                          start=(first and (h % 8) == 0),
                          stop=(last and (h % 8) == 7),
                          skip_group_check=True)
                te.mark(f"{tg}_kv_{s}")
        for e in ENGINES:
            G[e].mark("aa_end")

        ve.wait(ve.at("tensor", "aa_end"))
        ve.op("tensor_copy", kv_sb[:, 0:260], bank(4, 0, 128, 0, 260))
        ve.op("tensor_copy", kv_sb[:, 260:520], bank(5, 0, 128, 0, 260))
        ve.op("tensor_scalar", cc_sb[:, 0:520], kv_sb[:], bmask_t[:, 0:1], None, ALU.mult)
        ve.op("tensor_scalar", cc_sb[:, 520:1040], kv_sb[:], bmask_t[:, 1:2], None, ALU.mult)
        ve.mark("ab_ccsb")
        gp.wait(gp.at("vector", "ab_ccsb"))
        gp.op("dma_start", inc=16, out=cc_in.ap(), in_=cc_sb[:])
        gp.mark("ab_ccin")
        gp.wait(gp.at("gpsimd", "ab_ccin"))
        gp.op("collective_compute", "AllReduce", ALU.add,
              replica_groups=[[0, 1, 2, 3, 4, 5, 6, 7]],
              ins=[cc_in.ap()], outs=[cc_out.ap()], inc=1)
        gp.mark("ab_cc")
        gp.wait(gp.at("gpsimd", "ab_cc"))
        gp.op("dma_start", inc=16, out=cc_sb[:], in_=cc_out.ap())
        gp.wait(gp.at("tensor", "aa_end"))
        gp.op("dma_start", inc=16, out=wexp_buf[0][:, 0:KC, :],
              in_=wo_d.rearrange("(c p) m -> p c m", p=128))
        gp.mark("ab_rd")
        ve.wait(ve.at("gpsimd", "ab_rd"))
        ve.op("tensor_scalar", kv_my[:], cc_sb[:, 0:520], bmask_t[:, 0:1], None, ALU.mult)
        ve.op("scalar_tensor_tensor", kv_my[:], cc_sb[:, 520:1040], bmask_t[:, 1:2],
              kv_my[:], ALU.mult, ALU.add)
        ve.mark("ab_kvmy")
        sc.wait(sc.at("vector", "ab_kvmy"))
        sc.op("activation", kv_bf[:].rearrange("p a b -> p (a b)"), kv_my[:], AF.Copy)
        sc.mark("ab_kvbf")
        ve.wait(ve.at("scalar", "ab_kvbf"))
        ve.op("memset", den_l[:], 0.0)
        ve.op("memset", kvz[:].rearrange("p a b -> p (a b)"), 0.0)
        for h in range(NH):
            g_i = (h % 8) // 2
            ve.op("tensor_copy", den_l[(h % 2) * 64:(h % 2) * 64 + 64, h:h + 1],
                  kv_bf[(h % 2) * 64:(h % 2) * 64 + 64, h // 8, g_i * 65 + 64: g_i * 65 + 65])
            ve.op("tensor_copy",
                  kvz[(h % 2) * 64:(h % 2) * 64 + 64, h // 2, (h % 2) * 64:(h % 2) * 64 + 64],
                  kv_bf[(h % 2) * 64:(h % 2) * 64 + 64, h // 8, g_i * 65: g_i * 65 + 64])
        ve.mark("ab_denl")
        barrier(G, "ac")

        for t_i in range(NATT):
            tg = f"ac{t_i}"
            if t_i > 0:
                sy.wait(sy.at("tensor", f"ac{t_i-1}_omm_{KC-1}"))
            else:
                sy.wait(sy.at("vector", "ab_denl"))
            sy.op("dma_start", inc=16, out=qf_t[:],
                  in_=q_scr.rearrange("(c p) t -> p c t", p=128)[:, :, t_i * TAU:(t_i + 1) * TAU])
            sy.mark(f"{tg}_qld")
            te.wait(te.at("sync", f"{tg}_qld"))
            for c in range(KC):
                ba = 2 + (c % 2)
                if not (t_i == 0 and c < 2):
                    te.wait(te.at("vector", f"{tg}_asc_{c-2}" if c >= 2 else f"ac{t_i-1}_asc_{KC-2+c}"))
                te.op("matmul", bank(ba), kvz[:, c, :], qf_t[:, c, :],
                      inc=1, start=True, stop=True)
                te.mark(f"{tg}_amm_{c}")
                if c == 0:
                    if t_i > 0:
                        te.wait(te.at("vector", f"ac{t_i-1}_radd_3"))
                else:
                    te.wait(te.at("vector", f"{tg}_zc_{c-1}"))
                te.op("matmul", bank(0, 0, 2, 0, TAU), den_l[:, 2 * c:2 * c + 2], qf_t[:, c, :],
                      inc=1, start=True, stop=True)
                te.mark(f"{tg}_dmm_{c}")
                sc.wait(sc.at("tensor", f"{tg}_dmm_{c}"))
                if c > 0:
                    sc.wait(sc.at("vector", f"{tg}_zc_{c-1}"))
                sc.op("activation", den_sb, bank(0, 0, 2, 0, TAU), AF.Identity, bias=cst_t[0:2, 1:2])
                sc.mark(f"{tg}_den_{c}")
                ve.wait(ve.at("scalar", f"{tg}_den_{c}"))
                ve.op("reciprocal", rc32, den_sb)
                ve.op("tensor_copy", zc_t, rc32)
                ve.mark(f"{tg}_zc_{c}")
                te.wait(te.at("vector", f"{tg}_zc_{c}"))
                if t_i == 0 and c == 0:
                    pass
                elif c == 0:
                    te.wait(te.at("vector", f"ac{t_i-1}_radd_3"))
                if not (t_i == 0 and c == 0):
                    te.wait(te.at("vector", f"{tg}_zbev_{c-1}" if c >= 1 else f"ac{t_i-1}_zbev_{KC-1}"))
                te.op("matmul", bank(1, 0, 128, 0, TAU), sel2_t[:], zc_t,
                      inc=1, start=True, stop=True)
                te.mark(f"{tg}_zb_{c}")
                ve.wait(ve.at("tensor", f"{tg}_zb_{c}"))
                ve.op("tensor_copy", zb_t[c % 2][:], bank(1, 0, 128, 0, TAU))
                ve.mark(f"{tg}_zbev_{c}")
                ve.wait(ve.at("tensor", f"{tg}_amm_{c}"))
                ve.op("tensor_mul", asc_t[:, c, :],
                      bankpair(1)[:, (c % 2) * 512:(c % 2) * 512 + 512], zb_t[c % 2][:])
                ve.mark(f"{tg}_asc_{c}")
            for c in range(KC):
                bo = 4 + (c % 2)
                if c == 0:
                    te.wait(te.at("vector", f"{tg}_asc_{KC-1}"))
                if not (t_i == 0 and c < 2):
                    te.wait(te.at("scalar", f"{tg}_oev_{c-2}" if c >= 2 else f"ac{t_i-1}_oev_{KC-2+c}"))
                for k in range(KC):
                    te.op("matmul", bank(bo), wexp_buf[0][:, k, c * 128:(c + 1) * 128],
                          asc_t[:, k, :],
                          inc=(1 if k == KC - 1 else 0), start=(k == 0), stop=(k == KC - 1))
                te.mark(f"{tg}_omm_{c}")
                sc.wait(sc.at("tensor", f"{tg}_omm_{c}"))
                if c < 2 and t_i > 0:
                    sc.wait(sc.at("tensor", f"ac{t_i-1}_rtr_3"))
                sc.op("activation", of_t[:, c, :], bank(bo), AF.Identity,
                      bias=bias_t[:, 3 * KC + c: 3 * KC + c + 1])
                sc.mark(f"{tg}_oev_{c}")
            for s in range(4):
                ug = t_i * 4 + s
                if s == 0:
                    te.wait(te.at("scalar", f"{tg}_oev_{KC-1}"))
                if not (t_i == 0 and s == 0):
                    te.wait(te.at("vector", f"{tg}_radd_{s-1}" if s > 0 else f"ac{t_i-1}_radd_3"))
                for c in range(KC):
                    te.op("transpose", bankpair(0)[:, c * 128:(c + 1) * 128],
                          of_t[:, c, s * 128:(s + 1) * 128], idf_t[:],
                          inc=(1 if c == KC - 1 else 0))
                te.mark(f"{tg}_rtr_{s}")
                if s >= 2:
                    sy.wait(sy.at("vector", f"{tg}_radd_{s-2}"))
                elif t_i > 0:
                    sy.wait(sy.at("vector", f"ac{t_i-1}_radd_{2+s}"))
                sy.op("dma_start", inc=16, out=xr_tm[s % 2][:], in_=src[ug * 128:(ug + 1) * 128, :])
                sy.mark(f"{tg}_rx_{s}")
                ve.wait(ve.at("tensor", f"{tg}_rtr_{s}"))
                ve.wait(ve.at("sync", f"{tg}_rx_{s}"))
                if not (t_i == 0 and s < 2):
                    ve.wait(ve.at("sync", f"{tg}_rst_{s-2}" if s >= 2 else f"ac{t_i-1}_rst_{2+s}"))
                ve.op("tensor_add", o_tm[s % 2][:], bankpair(0), xr_tm[s % 2][:])
                ve.mark(f"{tg}_radd_{s}")
                sy.wait(sy.at("vector", f"{tg}_radd_{s}"))
                sy.op("dma_start", inc=16, out=dst[ug * 128:(ug + 1) * 128, :], in_=o_tm[s % 2][:])
                sy.mark(f"{tg}_rst_{s}")
        for e in ENGINES:
            G[e].mark("att_end")

    def schedule(G):
        sy, gp, sc, ve, te = (G[n] for n in ENGINES)
        sy.op("dma_start", inc=16, out=idf_t[:], in_=idf_d)
        sy.op("dma_start", inc=16, out=idb_t[:], in_=idb_d)
        sy.op("dma_start", inc=16, out=id32_t[:], in_=id32_d)
        sy.op("dma_start", inc=16, out=sel2_t[:], in_=sel2_d)
        sy.op("dma_start", inc=16,
              out=wrt_t[:],
              in_=wrt_d.rearrange("(c p) e -> p c e", p=128))
        sy.op("dma_start", inc=16, out=ebias_t[:], in_=ebias_d)
        sy.op("dma_start", inc=16, out=bias_t[:], in_=bias_d)
        sy.op("dma_start", inc=16, out=bmask_t[:], in_=bmask_d)
        sy.op("dma_start", inc=16, out=cst_t[:], in_=cst_d)
        sy.mark("prologue_dma")
        for e in ["vector", "scalar", "tensor", "gpsimd"]:
            G[e].wait(G[e].at("sync", "prologue_dma"))
        ve.op("memset", stat[:], 0.0)
        ve.op("memset", comb[32:34, :], 0.5)
        ve.mark("prologue")
        for _ in range(4):
            gp.op("nop")
        gp.mark("prologue_gp")

        if stage in ("ln1", "ln2"):
            cut = int(os.environ.get("KMK_CUT", "99"))
            tag = "f0s0"
            barrier(G, f"{tag}_a")
            for u in range(2):
                emit_ln(G, "f0ln", xin, u, xn_fm, u * 128, stage == "ln2", cut=cut)
                if u == 0:
                    ve.op("tensor_copy", uw_t[0][:, 0:16], stat[:])
                    ve.mark("statsnap")
            sy.wait(sy.at("vector", "statsnap"))
            sy.op("dma_start", inc=16, out=out_d[256:384, 0:16], in_=uw_t[0][:, 0:16])
            ve.wait(ve.at("scalar", "f0ln_evac_1"))
            ve.op("tensor_copy", wbc_t[:, 0:1024].rearrange("p (c t) -> p c t", t=128), xn_fm[:, :, 0:128])
            ve.mark("dump1")
            sy.wait(sy.at("vector", "dump1"))
            sy.op("dma_start", inc=16, out=out_d[0:128, :].rearrange("p (c t) -> p c t", t=128), in_=wbc_t[:, 0:1024].rearrange("p (c t) -> p c t", t=128))
            if stage == "ln2" and int(os.environ.get("KMK_CUT", "99")) >= 99:
                sy.wait(sy.at("vector", "f0ln_combem_1"))
                sy.op("dma_start", inc=16, out=out_d[128:162, 0:256], in_=comb[:, 0:256])
        elif stage == "fa" or stage == "fb1":
            tag = "f0s0"
            barrier(G, f"{tag}_a")
            for u in range(NU):
                emit_ln(G, "f0ln", xin, u, xn_fm, u * 128, True)
            barrier(G, f"{tag}_b")
            if stage == "fb1":
                emit_experts(G, tag, nloop=1)
                barrier(G, f"{tag}_c")
                sy.wait(sy.at("vector", f"{tag}_sweep_end"))
                sy.op("dma_start", inc=16,
                      out=out_d[0:2048, :].rearrange("(r p) m -> p r m", p=128),
                      in_=dacc[:].rearrange("p c t -> p (c t)").rearrange(
                          "p (r m) -> p r m", m=1024))
            else:
                sy.wait(sy.at("vector", f"f0ln_combem_{NU-1}"))
                sy.op("dma_start", inc=16, out=out_d[0:34, 0:1024], in_=comb[:, 0:1024])
                sy.op("dma_start", inc=16, out=out_d[34:68, 0:1024], in_=comb[:, 1024:2048])
        elif stage == "f1":
            emit_f_call(G, 0, xin, out_d)
        else:
            emit_f_call(G, 0, xin, x2_scr)
            if stage == "a":
                emit_attn(G, x2_scr, out_d)
            elif stage == "adbg":
                emit_attn(G, x2_scr, x3_scr)
                ve.wait(ve.at("vector", "att_end"))
                ve.op("tensor_copy", wbc_t[:, 0:520], kv_my[:, :])
                ve.mark("dbg1")
                sy.wait(sy.at("vector", "dbg1"))
                sy.op("dma_start", inc=16, out=out_d[0:128, 0:520], in_=wbc_t[:, 0:520])
                sy.op("dma_start", inc=16, out=qf_t[:],
                      in_=q_scr.rearrange("(c p) t -> p c t", p=128)[:, :, 0:TAU])
                sy.mark("dbgq")
                ve.wait(ve.at("sync", "dbgq"))
                ve.op("tensor_copy", wbc_t[:].rearrange("p (c t) -> p c t", t=TAU),
                      qf_t[:, 0:4, :])
                ve.mark("dbg2")
                sy.wait(sy.at("vector", "dbg2"))
                sy.op("dma_start", inc=16,
                      out=out_d[1024:1280, :].rearrange("(c2 p) m -> p c2 m", p=128),
                      in_=wbc_t[:].rearrange("p (c2 m) -> p c2 m", m=1024))
                sy.mark("dbg2x")
                ve.wait(ve.at("vector", "dbg2"))
                ve.wait(ve.at("sync", "dbg2x"))
                ve.op("tensor_copy", o_tm[0][:, 0:16], den_l[:, :])
                ve.mark("dbg3")
                sy.wait(sy.at("vector", "dbg3"))
                sy.op("dma_start", inc=16, out=out_d[256:384, 0:16], in_=o_tm[0][:, 0:16])
                ve.op("tensor_copy", o_tm[1][:].rearrange("p (c t) -> p c t", t=TAU)[:, 0:2, :],
                      asc_t[:, 0:2, :])
                ve.mark("dbg4")
                sy.wait(sy.at("vector", "dbg4"))
                sy.op("dma_start", inc=16, out=out_d[384:512, :], in_=o_tm[1][:, :])
                ve.op("tensor_copy", o_tm[0][:, 16:18].rearrange("p a -> a p") if False else o_tm[0][0:2, 16:528], den_sb)
                ve.mark("dbg5")
                sy.wait(sy.at("vector", "dbg5"))
                sy.op("dma_start", inc=16, out=out_d[1280:1282, 0:512], in_=o_tm[0][0:2, 16:528])
                sy.op("dma_start", inc=16, out=out_d[1284:1412, 0:512], in_=zb_t[1][:, :])
                sy.op("dma_start", inc=16, out=out_d[512:1024, :], in_=x3_scr[0:512, :])
            else:
                emit_attn(G, x2_scr, x3_scr)
                emit_f_call(G, 1, x3_scr, out_d)

        for e in ENGINES:
            G[e].mark("the_end")
        if sy.emit:
            sy.eng.wait_ge(sems["sync"], plan[("sync", "the_end")].a)

    G0 = {n: Eng(n, plan, False, record=True) for n in ENGINES}
    schedule(G0)
    totals = {n: G0[n].cnt for n in ENGINES}

    with nc.Block() as block:
        def mk_section(name):
            def section(eng):
                G = {n: Eng(n, plan, n == name, eng if n == name else None, sems)
                     for n in ENGINES}
                schedule(G)
                assert G[name].cnt == totals[name], f"{name}: {G[name].cnt} != {totals[name]}"
            return section

        block.sync(mk_section("sync"))
        block.gpsimd(mk_section("gpsimd"))
        block.scalar(mk_section("scalar"))
        block.vector(mk_section("vector"))
        block.tensor(mk_section("tensor"))

    return nc, totals


def _prep_inputs(inp):
    import ml_dtypes
    f = np.float32
    gupT = np.transpose(np.asarray(inp["L0_ffn_gate_up_proj"], f), (0, 2, 1))
    sg = np.asarray(inp["L0_ffn_shared_gate_w"], f).T
    su = np.asarray(inp["L0_ffn_shared_up_w"], f).T
    sd = np.asarray(inp["L0_ffn_shared_down_w"], f).T
    ps1 = np.concatenate([sg[:, 0:512], su[:, 0:512]], axis=1)[None]
    ps2 = np.concatenate([sg[:, 512:1024], su[:, 512:1024]], axis=1)[None]
    gup_all = np.ascontiguousarray(np.concatenate([gupT, ps1, ps2], axis=0))
    dwnT = np.transpose(np.asarray(inp["L0_ffn_down_proj"], f), (0, 2, 1))
    dwn_all = np.ascontiguousarray(
        np.concatenate([dwnT, sd[None, 0:512, :], sd[None, 512:1024, :]], axis=0))

    def lhsT(w):
        return np.ascontiguousarray(np.asarray(w, f).T)

    def bias_col(b):
        return np.ascontiguousarray(np.asarray(b, f).reshape(KC, 128).T)

    biases = np.concatenate([bias_col(inp["L0_attn_bq"]), bias_col(inp["L0_attn_bk"]),
                             bias_col(inp["L0_attn_bv"]), bias_col(inp["L0_attn_bo"])],
                            axis=1)

    wexp = np.ascontiguousarray(np.concatenate([gup_all, dwn_all], axis=1))
    common = {
        "wexp": wexp,
        "wq": lhsT(inp["L0_attn_Wq"]), "wk": lhsT(inp["L0_attn_Wk"]),
        "wv": lhsT(inp["L0_attn_Wv"]), "wo": lhsT(inp["L0_attn_Wo"]),
        "wrt": lhsT(inp["L0_ffn_router_weight"]),
        "bqkvo": np.ascontiguousarray(biases),
        "ebias": np.ascontiguousarray(
            np.broadcast_to(np.asarray(inp["L0_ffn_e_score_bias"], f), (128, 32))),
        "idf": np.eye(128, dtype=f),
        "idb": np.eye(128).astype(ml_dtypes.bfloat16),
        "id32": np.eye(32, dtype=f),
        "sel2": np.repeat(np.eye(2, dtype=f), 64, axis=1),
        "cst": np.ascontiguousarray(np.broadcast_to(
            np.array([EPS, 1e-6, -0.5, 0.0], f), (128, 4))),
    }
    x = np.asarray(inp["L0_attn_hidden_states"], dtype=np.float32)
    in_maps = []
    for c in range(8):
        b, s0 = c // 4, (c % 4) * TC
        m = dict(common)
        m["xin"] = np.ascontiguousarray(x[b, s0:s0 + TC, :])
        bm = np.zeros((128, 2), f)
        bm[:, b] = 1.0
        m["bmask"] = bm
        in_maps.append(m)
    return in_maps


_CACHE = {}


def kernel(**inputs):
    stage = os.environ.get("KMK_STAGE", "full")
    in_maps = _prep_inputs(inputs)
    if stage not in _CACHE:
        _CACHE[stage] = build_program(stage)
    nc, _ = _CACHE[stage]
    res = run_bass_kernel_spmd(nc, in_maps, core_ids=list(range(8)))
    full = np.empty((2, 16384, D), np.float32)
    for c in range(8):
        full[c // 4, (c % 4) * TC:(c % 4 + 1) * TC, :] = res.results[c]["out"]
    return full


# version 05
# speedup vs baseline: 1.1345x; 1.1345x over previous
import os
import numpy as np
from contextlib import ExitStack

import concourse.bass as bass
import concourse.mybir as mybir
from concourse.bass_utils import run_bass_kernel_spmd

F32 = mybir.dt.float32
F32R = mybir.dt.float32r
BF16 = mybir.dt.bfloat16
AF = mybir.ActivationFunctionType
ALU = mybir.AluOpType
AX = mybir.AxisListType.X

D = 1024
KC = 8
TC = 4096
ST = 2048
NST = TC // ST
TAU = 512
NTAU = ST // TAU
NU = ST // 128
NE = 34
NH = 16
EPS = 1e-5
NATT = TC // TAU
NLOOP = NE // 2


class Aff:
    __slots__ = ("a", "b")

    def __init__(self, a, b=0):
        self.a, self.b = a, b


class Eng:
    _uid = [0]

    def __init__(self, name, plan, emit, eng=None, sems=None, record=False):
        self.name = name
        self.plan = plan
        self.emit = emit
        self.record = record
        self.eng = eng
        self.sems = sems
        self.cnt = 0
        self.coef = 0
        self.loop_var = None
        self.in_loop = False
        self._loop_marks = []
        self._dma_id = 0
        self._ss_id = 0

    def mark(self, key):
        k = (self.name, key)
        a = self.cnt + self.coef if self.in_loop else self.cnt
        if self.record:
            if self.in_loop:
                m = Aff(a, -1)
                self.plan[k] = m
                self._loop_marks.append(k)
            else:
                self.plan[k] = Aff(a, 0)
        else:
            p = self.plan[k]
            assert p.a == a, f"drift {k}: plan={p.a} now={a}"

    def at(self, eng_name, key, delta=0, prev_iter=False):
        return (eng_name, key, delta, prev_iter)

    def wait(self, target):
        if not self.emit:
            return
        eng_name, key, delta, prev_iter = target
        aff = self.plan[(eng_name, key)]
        a, b = aff.a + delta, aff.b
        assert b >= 0, f"unpatched milestone {(eng_name, key)}"
        if prev_iter:
            a -= b
        if b == 0 or not self.in_loop:
            assert b == 0, f"affine milestone {(eng_name, key)} used outside loop"
            if a <= 0:
                return
            self.eng.wait_ge(self.sems[eng_name], a)
        else:
            assert self.loop_var is not None
            assert a >= 0, f"negative wait base {a} for {(eng_name, key)}"
            if b not in self._rcoef:
                Eng._uid[0] += 1
                r = self.eng.alloc_register(f"rc_{self.name}_{b}_{Eng._uid[0]}")
                self.eng.reg_alu(r, self.loop_var, b, op=ALU.mult)
                self._rcoef[b] = r
            self.eng.reg_alu(self._rtmp, self._rcoef[b], a, op=ALU.add)
            self.eng.wait_ge(self.sems[eng_name], self._rtmp)

    def op(self, fn_name, *args, inc=1, **kwargs):
        if fn_name in ("dma_start", "collective_compute") or                 self.name in ("vector", "scalar", "gpsimd"):
            self._dma_id += 1
            key = f"__dma_{self.name}_{self._dma_id}"
            self.mark(key)
            self.wait(self.at(self.name, key))
        if self.emit:
            ins = getattr(self.eng, fn_name)(*args, **kwargs)
            if inc:
                ins.then_inc(self.sems[self.name], inc)
        if inc:
            if self.in_loop:
                self.coef += inc
            else:
                self.cnt += inc

    def selfsync(self):
        self._ss_id += 1
        key = f"__ss_{self.name}_{self._ss_id}"
        self.mark(key)
        self.wait(self.at(self.name, key))

    def loop(self, trip):
        outer = self

        class L:
            def __enter__(self_l):
                outer.in_loop = True
                outer.coef = 0
                outer._loop_marks = []
                outer._rcoef = {}
                if outer.emit:
                    self_l._cm = outer.eng.Fori(0, trip)
                    outer.loop_var = self_l._cm.__enter__()
                    Eng._uid[0] += 1
                    outer._rtmp = outer.eng.alloc_register(f"rt_{outer.name}_{Eng._uid[0]}")
                return outer.loop_var

            def __exit__(self_l, *exc):
                if outer.emit:
                    for r in outer._rcoef.values():
                        outer.eng.free_register(r)
                    outer.eng.free_register(outer._rtmp)
                    self_l._cm.__exit__(*exc)
                if outer.record:
                    for k in outer._loop_marks:
                        outer.plan[k].b = outer.coef
                outer.cnt += outer.coef * trip
                outer.coef = 0
                outer.in_loop = False
                outer.loop_var = None
                return False
        return L()


ENGINES = ["sync", "gpsimd", "scalar", "vector", "tensor"]


def build_program(stage="full"):
    nc = bass.Bass("TRN2", target_bir_lowering=False)

    xin = nc.dram_tensor("xin", [TC, D], F32, kind="ExternalInput").ap()
    wexp_d = nc.dram_tensor("wexp", [NE, 1536, D], F32, kind="ExternalInput").ap()
    wq_d = nc.dram_tensor("wq", [D, D], F32, kind="ExternalInput").ap()
    wk_d = nc.dram_tensor("wk", [D, D], F32, kind="ExternalInput").ap()
    wv_d = nc.dram_tensor("wv", [D, D], F32, kind="ExternalInput").ap()
    wo_d = nc.dram_tensor("wo", [D, D], F32, kind="ExternalInput").ap()
    wrt_d = nc.dram_tensor("wrt", [D, 32], F32R, kind="ExternalInput").ap()
    bias_d = nc.dram_tensor("bqkvo", [128, 4 * KC], F32, kind="ExternalInput").ap()
    ebias_d = nc.dram_tensor("ebias", [128, 32], F32, kind="ExternalInput").ap()
    idf_d = nc.dram_tensor("idf", [128, 128], F32, kind="ExternalInput").ap()
    idb_d = nc.dram_tensor("idb", [128, 128], BF16, kind="ExternalInput").ap()
    id32_d = nc.dram_tensor("id32", [32, 32], F32, kind="ExternalInput").ap()
    sel2_d = nc.dram_tensor("sel2", [2, 128], F32R, kind="ExternalInput").ap()
    bmask_d = nc.dram_tensor("bmask", [128, 2], F32, kind="ExternalInput").ap()
    cst_d = nc.dram_tensor("cst", [128, 4], F32, kind="ExternalInput").ap()

    out_d = nc.dram_tensor("out", [TC, D], F32, kind="ExternalOutput").ap()
    x2_scr = nc.dram_tensor("x2_scr", [TC, D], F32).ap()
    x3_scr = nc.dram_tensor("x3_scr", [TC, D], F32).ap()
    q_scr = nc.dram_tensor("q_scr", [D, TC], BF16).ap()
    comb_dram = nc.dram_tensor("comb_dram", [NE, ST], F32).ap()
    cc_in = nc.dram_tensor("cc_in", [128, 4 * 260], F32)
    cc_out = nc.dram_tensor("cc_out", [128, 4 * 260], F32)

    ctx = ExitStack()

    def sbt(name, shape, dtype):
        return ctx.enter_context(nc.sbuf_tensor(name, shape, dtype))

    idf_t = sbt("idf_t", [128, 128], F32)
    id32_t = sbt("id32_t", [32, 32], F32)
    idb_t = sbt("idb_t", [128, 128], BF16)
    sel2_t = sbt("sel2_t", [2, 128], F32R)
    wrt_t = sbt("wrt_t", [128, KC, 32], F32R)
    ebias_t = sbt("ebias_t", [128, 32], F32)
    bias_t = sbt("bias_t", [128, 4 * KC], F32)
    bmask_t = sbt("bmask_t", [128, 2], F32)
    cst_t = sbt("cst_t", [128, 4], F32)

    wexp_buf = [sbt(f"wexp{i}", [128, 12, D], BF16) for i in range(2)]

    xn_fm = sbt("xn_fm", [128, KC, ST], BF16)
    dacc = sbt("dacc", [128, KC, ST], F32)
    comb = sbt("comb", [NE, ST], F32)
    wbc_t = sbt("wbc", [128, ST], F32)
    wbc = [wbc_t, wbc_t]
    x_tm = [sbt(f"x_tm{i}", [128, D], F32) for i in range(2)]
    xr_tm = x_tm
    o_tm = [sbt(f"o_tm{i}", [128, D], F32) for i in range(2)]
    xn_tm = sbt("xn_tm", [128, D], F32)
    xn32 = sbt("xn32", [128, KC, 128], F32R)
    stat = sbt("stat", [128, 16], F32)
    lg_s = sbt("lg_s", [32, 128], F32)
    lg_tm = sbt("lg_tm", [128, 32], F32)
    ex_t = sbt("ex_t", [128, 32], F32)
    rw_t = sbt("rw_t", [128, 32], F32)
    corr_t = sbt("corr_t", [128, 32], F32)
    m8_t = sbt("m8_t", [128, 8], F32)
    mask_t = sbt("mask_t", [128, 32], F32)
    comb_tm = sbt("comb_tm", [128, 32], F32)

    sg32 = [sbt(f"sg32_{i}", [128, TAU], F32) for i in range(2)]
    uw_t = [sbt(f"uw_{i}", [128, TAU], F32) for i in range(2)]
    hmid = sbt("hmid", [128, 4, TAU], BF16)

    def carve(base, byte_off, shape, dtype):
        tot = 1
        for d in base.shape[1:]:
            tot *= d
        h = base.reshape([128, tot]).bitcast(dtype)
        esz = {F32: 4, F32R: 4, BF16: 2}[dtype]
        n = 1
        for d in shape[1:]:
            n *= d
        assert byte_off % esz == 0
        e0 = byte_off // esz
        ap = h[:, e0:e0 + n]
        if len(shape) == 3:
            ap = ap.rearrange("p (a b) -> p a b", b=shape[2])
        return ap

    of_t = carve(dacc, 0, [128, KC, TAU], F32)
    asc_t = carve(dacc, 16384, [128, KC, TAU], BF16)
    qf_t = carve(dacc, 24576, [128, KC, TAU], BF16)
    kb_t = carve(dacc, 32768, [128, KC, TAU], BF16)
    vb_t = carve(dacc, 40960, [128, KC, TAU], BF16)
    xn_a = carve(dacc, 49152, [128, KC, TAU], BF16)
    cc_sb = carve(dacc, 57344, [128, 4 * 260], F32)
    z_ch = [carve(xn_fm, 0 + i * 2048, [128, TAU], F32) for i in range(2)]
    r_ch = [carve(xn_fm, 4096 + i * 2048, [128, TAU], F32) for i in range(2)]
    mn_ch = [carve(xn_fm, 8192 + i * 2048, [128, TAU], F32) for i in range(2)]

    kv_sb = carve(xn_fm, 20480, [128, 2 * 260], F32)
    kv_my = carve(xn_fm, 22560, [128, 2 * 260], F32)
    k_tm = carve(xn_fm, 24640, [128, D], BF16)
    v_aug = carve(xn_fm, 26688, [128, NH * 65], BF16)
    kv_bf = carve(xn_fm, 28768, [128, 2, 260], BF16)
    kvz = carve(dacc, 61504, [128, KC, 128], BF16)
    den_l = carve(xn_fm, 29808, [128, 16], BF16)
    zb_t = [carve(xn_fm, 12288 + i * 2048, [128, TAU], F32) for i in range(2)]
    den_sb = carve(xn_fm, 16384, [128, TAU], F32)[0:2, :]
    zc_full = sbt("zc_t", [2, TAU], F32R)
    zc_t = zc_full[0:2, :]
    rc32 = carve(xn_fm, 29952, [128, TAU], F32)[0:2, :]

    pctx = ExitStack()
    pbl = [pctx.enter_context(nc.psum_tensor(f"pbl{i}", [128, 1024], F32)) for i in range(3)]
    pbf = pctx.enter_context(nc.psum_tensor("pbf", [128, 2048], BF16))

    sem_ctx = ExitStack()
    sems = {n: sem_ctx.enter_context(nc.semaphore(f"s_{n}")) for n in ENGINES}

    plan = {}

    def bank(i, p0=0, np_=128, c0=0, nc_=512):
        t = pbl[i // 2]
        off = (i % 2) * 512
        return t[p0:p0 + np_, off + c0: off + c0 + nc_]

    def bankpair(j):
        return pbl[j][:, :]

    def barrier(G, name, waiters=("tensor", "vector", "scalar", "sync")):
        markers = ENGINES
        for m in markers:
            G[m].mark(f"bar_{name}")
        for w in waiters:
            for m in markers:
                if m != w:
                    G[w].wait(G[w].at(m, f"bar_{name}"))

    def emit_ln(G, tag, x_src, ug, dst_fm, dst_off, want_router, st_off=0, force_j=None, cut=99):
        sy, gp, sc, ve, te = (G[n] for n in ENGINES)
        xb = x_tm[ug % 2]

        if ug >= 2:
            sy.wait(sy.at("vector", f"{tag}_xn_{ug-2}"))
            sy.wait(sy.at("scalar", f"{tag}_sq_{ug-2}"))
        sy.op("dma_start", inc=16, out=xb[:], in_=x_src[ug * 128:(ug + 1) * 128, :])
        sy.mark(f"{tag}_x_{ug}")

        ve.wait(ve.at("sync", f"{tag}_x_{ug}"))
        ve.op("reduce_sum", stat[:, 0:1], xb[:], axis=AX, negate=True)
        sc.wait(sc.at("sync", f"{tag}_x_{ug}"))
        if ug >= 1:
            sc.wait(sc.at("vector", f"{tag}_var_{ug-1}"))
            sc.wait(sc.at("tensor", f"{tag}_tr_{ug-1}"))
        sc.op("activation", xn_tm[:], xb[:], AF.Square)
        sc.mark(f"{tag}_sq_{ug}")
        ve.wait(ve.at("scalar", f"{tag}_sq_{ug}"))
        ve.op("reduce_sum", stat[:, 1:2], xn_tm[:], axis=AX)
        ve.selfsync()
        ve.op("tensor_scalar", stat[:, 2:3], stat[:, 0:1], 1.0 / D, None, ALU.mult)
        ve.mark(f"{tag}_s1_{ug}")
        ve.wait(ve.at("vector", f"{tag}_s1_{ug}"))
        ve.op("tensor_mul", stat[:, 10:11], stat[:, 2:3], stat[:, 2:3])
        ve.mark(f"{tag}_msq_{ug}")
        ve.wait(ve.at("vector", f"{tag}_msq_{ug}"))
        ve.op("scalar_tensor_tensor", stat[:, 3:4], stat[:, 1:2], 1.0 / D, stat[:, 10:11],
              ALU.mult, ALU.subtract)
        ve.mark(f"{tag}_var_{ug}")
        sc.wait(sc.at("vector", f"{tag}_var_{ug}"))
        sc.op("activation", stat[:, 11:12], stat[:, 3:4], AF.Ln, bias=cst_t[:, 0:1])
        sc.selfsync()
        sc.op("activation", stat[:, 4:5], stat[:, 11:12], AF.Exp, scale=cst_t[:, 2:3])
        sc.mark(f"{tag}_rstd_{ug}")
        ve.wait(ve.at("scalar", f"{tag}_rstd_{ug}"))
        ve.op("tensor_scalar", xn_tm[:], xb[:], stat[:, 2:3], stat[:, 4:5], ALU.add, ALU.mult)
        ve.mark(f"{tag}_xn_{ug}")

        j = (ug % 2) if force_j is None else force_j
        back = 2 if force_j is None else 1
        te.wait(te.at("vector", f"{tag}_xn_{ug}"))
        if ug >= back:
            te.wait(te.at("scalar", f"{tag}_evac_{ug-back}"))
            if want_router:
                te.wait(te.at("vector", f"{tag}_evac32_{ug-back}"))
        for c in range(KC):
            te.op("transpose", bankpair(j)[:, c * 128:(c + 1) * 128],
                  xn_tm[:, c * 128:(c + 1) * 128], idf_t[:],
                  inc=(1 if c == KC - 1 else 0))
        te.mark(f"{tag}_tr_{ug}")

        sc.wait(sc.at("tensor", f"{tag}_tr_{ug}"))
        sc.op("activation",
              dst_fm[:, :, dst_off:dst_off + 128],
              bankpair(j).rearrange("p (c t) -> p c t", t=128), AF.Copy)
        sc.mark(f"{tag}_evac_{ug}")
        if not want_router:
            return

        ve.wait(ve.at("tensor", f"{tag}_tr_{ug}"))
        ve.wait(ve.at("scalar", f"{tag}_evac_{ug}"))
        ve.op("tensor_copy", xn32[:].rearrange("p c t -> p (c t)"), bankpair(j))
        ve.mark(f"{tag}_evac32_{ug}")
        if cut <= 0:
            return

        te.wait(te.at("vector", f"{tag}_evac32_{ug}"))
        if ug >= 1:
            te.wait(te.at("vector", f"{tag}_lgs_{ug-1}"))
        for c in range(KC):
            te.op("matmul", bank(4, 0, 32, 0, 128), wrt_t[:, c, :], xn32[:, c, :],
                  inc=(1 if c == KC - 1 else 0), start=(c == 0), stop=(c == KC - 1))
        te.mark(f"{tag}_rmm_{ug}")
        ve.wait(ve.at("tensor", f"{tag}_rmm_{ug}"))
        ve.op("tensor_copy", lg_s[:], bank(4, 0, 32, 0, 128))
        ve.mark(f"{tag}_lgs_{ug}")
        if cut <= 1:
            return
        te.wait(te.at("vector", f"{tag}_lgs_{ug}"))
        if ug >= 1:
            te.wait(te.at("vector", f"{tag}_combem_{ug-1}"))
        te.op("transpose", bank(5, 0, 128, 0, 32), lg_s[:], id32_t[:], inc=1)
        te.mark(f"{tag}_lgtr_{ug}")
        ve.wait(ve.at("tensor", f"{tag}_lgtr_{ug}"))
        ve.op("tensor_copy", lg_tm[:], bank(5, 0, 128, 0, 32))
        if cut <= 2:
            return
        ve.op("reduce_max", stat[:, 5:6], lg_tm[:], axis=AX, negate=True)
        ve.mark(f"{tag}_nmax_{ug}")
        sc.wait(sc.at("vector", f"{tag}_nmax_{ug}"))
        sc.op("activation", ex_t[:], lg_tm[:], AF.Exp, bias=stat[:, 5:6])
        sc.mark(f"{tag}_exp_{ug}")
        ve.wait(ve.at("scalar", f"{tag}_exp_{ug}"))
        ve.op("reduce_sum", stat[:, 6:7], ex_t[:], axis=AX)
        ve.mark(f"{tag}_rsum_{ug}")
        ve.wait(ve.at("vector", f"{tag}_rsum_{ug}"))
        ve.op("reciprocal", stat[:, 7:8], stat[:, 6:7])
        ve.mark(f"{tag}_rinv_{ug}")
        if cut <= 3:
            return
        ve.wait(ve.at("vector", f"{tag}_rinv_{ug}"))
        ve.op("tensor_scalar", rw_t[:], ex_t[:], stat[:, 7:8], None, ALU.mult)
        ve.op("tensor_add", corr_t[:], rw_t[:], ebias_t[:])
        ve.op("max", out=m8_t[:], in_=corr_t[:])
        ve.mark(f"{tag}_m8_{ug}")
        if cut <= 4:
            return
        ve.wait(ve.at("vector", f"{tag}_m8_{ug}"))
        ve.op("tensor_scalar", mask_t[:], corr_t[:], m8_t[:, 5:6], None, ALU.is_ge)
        ve.op("tensor_mul", rw_t[:], rw_t[:], mask_t[:])
        ve.op("reduce_sum", stat[:, 8:9], rw_t[:], axis=AX)
        ve.mark(f"{tag}_wsum_{ug}")
        ve.wait(ve.at("vector", f"{tag}_wsum_{ug}"))
        ve.op("tensor_scalar", stat[:, 12:13], stat[:, 8:9], 1e-12, None, ALU.max)
        ve.mark(f"{tag}_wclip_{ug}")
        ve.wait(ve.at("vector", f"{tag}_wclip_{ug}"))
        ve.op("reciprocal", stat[:, 9:10], stat[:, 12:13])
        ve.mark(f"{tag}_winv_{ug}")
        ve.wait(ve.at("vector", f"{tag}_winv_{ug}"))
        ve.op("tensor_scalar", comb_tm[:], rw_t[:], stat[:, 9:10], 0.5, ALU.mult, ALU.mult)
        ve.mark(f"{tag}_comb_{ug}")
        te.wait(te.at("vector", f"{tag}_comb_{ug}"))
        te.op("transpose", bank(5, 0, 32, 0, 128), comb_tm[:], idf_t[:], inc=1)
        te.mark(f"{tag}_combtr_{ug}")
        ve.wait(ve.at("tensor", f"{tag}_combtr_{ug}"))
        ve.op("tensor_copy", comb[0:32, st_off + dst_off: st_off + dst_off + 128],
              bank(5, 0, 32, 0, 128))
        ve.mark(f"{tag}_combem_{ug}")

    def emit_experts(G, tag, nloop=NLOOP):
        sy, gp, sc, ve, te = (G[n] for n in ENGINES)

        ve.op("memset", dacc[:].rearrange("p c t -> p (c t)"), 0.0)
        ve.mark(f"{tag}_dacc0")
        te.wait(te.at("vector", f"{tag}_dacc0"))

        def prev_ref(par, t_i, idx, n_idx):
            flat = (par * NTAU + t_i) * n_idx + idx
            return flat

        def expert_body(par):
            for t_i in range(NTAU):
                for p in range(4):
                    bg, bu = 2 * (p % 2), 2 * (p % 2) + 1
                    pp, ppar, pti = p - 2, par, t_i
                    if pp < 0:
                        pp += 4
                        pti -= 1
                        if pti < 0:
                            pti += NTAU
                            ppar = 1 - ppar
                    prev = (par == 0 and t_i == 0 and p < 2)
                    te.wait(te.at("scalar", f"{tag}_silu_{ppar}_{pti}_{pp}", prev_iter=prev))
                    te.wait(te.at("vector", f"{tag}_uw_{ppar}_{pti}_{pp}", prev_iter=prev))
                    for k in range(KC):
                        te.op("matmul", bank(bg), wexp_buf[par][:, k, p * 128:(p + 1) * 128],
                              xn_fm[:, k, t_i * TAU:(t_i + 1) * TAU],
                              inc=0, start=(k == 0), stop=(k == KC - 1))
                    for k in range(KC):
                        te.op("matmul", bank(bu),
                              wexp_buf[par][:, k, 512 + p * 128: 512 + (p + 1) * 128],
                              xn_fm[:, k, t_i * TAU:(t_i + 1) * TAU],
                              inc=(1 if k == KC - 1 else 0), start=(k == 0), stop=(k == KC - 1))
                    te.mark(f"{tag}_pair_{par}_{t_i}_{p}")
                    sc.wait(sc.at("tensor", f"{tag}_pair_{par}_{t_i}_{p}"))
                    sc.wait(sc.at("vector", f"{tag}_hm_{ppar}_{pti}_{pp}", prev_iter=prev))
                    sc.op("activation", sg32[p % 2][:], bank(bg), AF.Silu)
                    sc.mark(f"{tag}_silu_{par}_{t_i}_{p}")
                    ve.wait(ve.at("tensor", f"{tag}_pair_{par}_{t_i}_{p}"))
                    ve.wait(ve.at("sync", f"{tag}_wbc_{par}"))
                    ve.op("tensor_mul", uw_t[p % 2][:], bank(bu), wbc[par][:, t_i * TAU:(t_i + 1) * TAU])
                    ve.mark(f"{tag}_uw_{par}_{t_i}_{p}")
                    ve.wait(ve.at("scalar", f"{tag}_silu_{par}_{t_i}_{p}"))
                    ve.op("tensor_mul", hmid[:, p, :], sg32[p % 2][:], uw_t[p % 2][:])
                    ve.mark(f"{tag}_hm_{par}_{t_i}_{p}")
                for h in range(KC):
                    bd = 4 + (h % 2)
                    hh, hpar, hti = h - 2, par, t_i
                    if hh < 0:
                        hh += KC
                        hti -= 1
                        if hti < 0:
                            hti += NTAU
                            hpar = 1 - hpar
                    prev = (par == 0 and t_i == 0 and h < 2)
                    te.wait(te.at("vector", f"{tag}_dn_{hpar}_{hti}_{hh}", prev_iter=prev))
                    for k in range(4):
                        if h == 0:
                            te.wait(te.at("vector", f"{tag}_hm_{par}_{t_i}_{k}"))
                        te.op("matmul", bank(bd), wexp_buf[par][:, 8 + k, h * 128:(h + 1) * 128],
                              hmid[:, k, :],
                              inc=(1 if k == 3 else 0), start=(k == 0), stop=(k == 3))
                    te.mark(f"{tag}_dnmm_{par}_{t_i}_{h}")
                    ve.wait(ve.at("tensor", f"{tag}_dnmm_{par}_{t_i}_{h}"))
                    ve.op("scalar_tensor_tensor",
                          dacc[:, h, t_i * TAU:(t_i + 1) * TAU],
                          bank(bd), 1.0, dacc[:, h, t_i * TAU:(t_i + 1) * TAU],
                          ALU.mult, ALU.add)
                    ve.mark(f"{tag}_dn_{par}_{t_i}_{h}")

        sy.op("dma_start", inc=16, out=comb_dram, in_=comb[:])
        sy.mark(f"{tag}_combd")
        with sy.loop(nloop), gp.loop(nloop), sc.loop(nloop), ve.loop(nloop), te.loop(nloop):
            for par in range(2):
                gp.wait(gp.at("tensor", f"{tag}_dnmm_{par}_3_7", prev_iter=True))
                if gp.emit:
                    e_expr = gp.loop_var * 2 + par
                    gp.op("dma_start", inc=16,
                          out=wexp_buf[par][:],
                          in_=wexp_d[e_expr].rearrange("(c p) m -> p c m", p=128))
                else:
                    gp.op("dma_start", inc=16)
                gp.mark(f"{tag}_wld_{par}")
            for par in range(2):
                ppar_w = 1 - par
                prev_w = (par == 0)
                sy.wait(sy.at("vector", f"{tag}_uw_{ppar_w}_3_3", prev_iter=prev_w))
                if sy.emit:
                    e_expr = sy.loop_var * 2 + par
                    sy.op("dma_start", inc=16, out=wbc[par][:],
                          in_=comb_dram[bass.ts(e_expr, 1), :].broadcast_to([128, ST]))
                else:
                    sy.op("dma_start", inc=16)
                sy.mark(f"{tag}_wbc_{par}")
            for par in range(2):
                te.wait(te.at("gpsimd", f"{tag}_wld_{par}"))
                expert_body(par)
        for e in ENGINES:
            G[e].mark(f"{tag}_sweep_end")

    def emit_resid(G, tag, x_src, dst, st):
        sy, gp, sc, ve, te = (G[n] for n in ENGINES)
        for u in range(NU):
            ug = st * NU + u
            j = u % 2
            if u == 0:
                te.wait(te.at("vector", f"{tag}_sweep_end"))
                te.wait(te.at("scalar", f"{tag}_sweep_end"))
                te.wait(te.at("gpsimd", f"{tag}_sweep_end"))
            if u >= 2:
                te.wait(te.at("vector", f"{tag}_radd_{u-2}"))
            for c in range(KC):
                te.op("transpose", bankpair(j)[:, c * 128:(c + 1) * 128],
                      dacc[:, c, u * 128:(u + 1) * 128], idf_t[:],
                      inc=(1 if c == KC - 1 else 0))
            te.mark(f"{tag}_rtr_{u}")
            if u >= 2:
                sc.wait(sc.at("vector", f"{tag}_radd_{u-2}"))
            sc.op("dma_start", inc=16, out=xr_tm[j][:], in_=x_src[ug * 128:(ug + 1) * 128, :])
            sc.mark(f"{tag}_rx_{u}")
            ve.wait(ve.at("tensor", f"{tag}_rtr_{u}"))
            ve.wait(ve.at("scalar", f"{tag}_rx_{u}"))
            if u >= 2:
                ve.wait(ve.at("sync", f"{tag}_rst_{u-2}"))
            ve.op("tensor_add", o_tm[j][:], bankpair(j), xr_tm[j][:])
            ve.mark(f"{tag}_radd_{u}")
            sy.wait(sy.at("vector", f"{tag}_radd_{u}"))
            sy.op("dma_start", inc=16, out=dst[ug * 128:(ug + 1) * 128, :], in_=o_tm[j][:])
            sy.mark(f"{tag}_rst_{u}")

    def emit_f_call(G, cid, x_src, dst):
        for st in range(NST):
            tag = f"f{cid}s{st}"
            barrier(G, f"{tag}_a")
            for u in range(NU):
                emit_ln(G, f"f{cid}ln", x_src, st * NU + u, xn_fm, u * 128, True)
            barrier(G, f"{tag}_b")
            emit_experts(G, tag)
            barrier(G, f"{tag}_c")
            emit_resid(G, tag, x_src, dst, st)

    def emit_attn(G, src, dst):
        sy, gp, sc, ve, te = (G[n] for n in ENGINES)
        barrier(G, "att_w")
        gp.wait(gp.at("tensor", "bar_att_w"))
        gp.op("dma_start", inc=16, out=wexp_buf[0][:, 0:KC, :],
              in_=wq_d.rearrange("(c p) m -> p c m", p=128))
        gp.op("dma_start", inc=16, out=wexp_buf[1][:, 0:KC, :],
              in_=wk_d.rearrange("(c p) m -> p c m", p=128))
        gp.op("dma_start", inc=16, out=wexp_buf[0][:, 8:12, :],
              in_=wv_d[0:512].rearrange("(c p) m -> p c m", p=128))
        gp.op("dma_start", inc=16, out=wexp_buf[1][:, 8:12, :],
              in_=wv_d[512:1024].rearrange("(c p) m -> p c m", p=128))
        gp.mark("att_wld")
        te.wait(te.at("gpsimd", "att_wld"))
        ve.op("memset", v_aug[:], 1.0)
        ve.mark("att_vaug")

        for t_i in range(NATT):
            tg = f"at{t_i}"
            for s in range(4):
                emit_ln(G, "aln", src, t_i * 4 + s, xn_a, s * 128, False, force_j=0)
            for w_i in range(3):
                for c in range(KC):
                    bq = 2 + (c % 2)
                    if w_i == 0 and c == 0:
                        te.wait(te.at("scalar", f"aln_evac_{t_i*4+3}"))
                    if w_i == 0 and c < 2:
                        if t_i > 0:
                            te.wait(te.at("scalar", f"at{t_i-1}_zev_2_{KC-2+c}"))
                    elif c < 2:
                        te.wait(te.at("scalar", f"{tg}_zev_{w_i-1}_{KC-2+c}"))
                    else:
                        te.wait(te.at("scalar", f"{tg}_zev_{w_i}_{c-2}"))
                    for k in range(KC):
                        if w_i < 2:
                            lhs = wexp_buf[w_i][:, k, c * 128:(c + 1) * 128]
                        else:
                            lhs = wexp_buf[k // 4][:, 8 + k % 4, c * 128:(c + 1) * 128]
                        te.op("matmul", bank(bq), lhs, xn_a[:, k, :],
                              inc=(1 if k == KC - 1 else 0), start=(k == 0), stop=(k == KC - 1))
                    te.mark(f"{tg}_mm_{w_i}_{c}")
                    sc.wait(sc.at("tensor", f"{tg}_mm_{w_i}_{c}"))
                    if w_i < 2:
                        if c >= 2:
                            sc.wait(sc.at("vector", f"{tg}_fma_{w_i}_{c-2}"))
                        elif w_i == 1:
                            sc.wait(sc.at("vector", f"{tg}_fma_0_{KC-2+c}"))
                        elif t_i > 0:
                            sc.wait(sc.at("vector", f"at{t_i-1}_fma_1_{KC-2+c}"))
                        sc.op("activation", z_ch[c % 2][:], bank(bq), AF.Identity,
                              bias=bias_t[:, w_i * KC + c: w_i * KC + c + 1])
                    else:
                        if c < 2 and t_i > 0:
                            sc.wait(sc.at("tensor", f"at{t_i-1}_vtr_3"))
                        sc.op("activation", vb_t[:, c, :], bank(bq), AF.Identity,
                              bias=bias_t[:, 2 * KC + c: 2 * KC + c + 1])
                    sc.mark(f"{tg}_zev_{w_i}_{c}")
                    if w_i < 2:
                        ve.wait(ve.at("scalar", f"{tg}_zev_{w_i}_{c}"))
                        if w_i == 0 and c < 2 and t_i > 0:
                            ve.wait(ve.at("sync", f"at{t_i-1}_qst"))
                            ve.wait(ve.at("tensor", f"at{t_i-1}_ktr_3"))
                        ve.op("tensor_scalar", r_ch[c % 2][:], z_ch[c % 2][:], 0.0, None, ALU.max)
                        ve.op("tensor_sub", mn_ch[c % 2][:], z_ch[c % 2][:], r_ch[c % 2][:])
                        ve.mark(f"{tg}_fma_{w_i}_{c}")
                        sc.wait(sc.at("vector", f"{tg}_fma_{w_i}_{c}"))
                        if c >= 2:
                            sc.wait(sc.at("vector", f"{tg}_fmc_{w_i}_{c-2}"))
                        elif w_i == 1:
                            sc.wait(sc.at("vector", f"{tg}_fmc_0_{KC-2+c}"))
                        elif t_i > 0:
                            sc.wait(sc.at("vector", f"at{t_i-1}_fmc_1_{KC-2+c}"))
                        sc.op("activation", mn_ch[c % 2][:], mn_ch[c % 2][:], AF.Exp)
                        sc.mark(f"{tg}_fmb_{w_i}_{c}")
                        ve.wait(ve.at("scalar", f"{tg}_fmb_{w_i}_{c}"))
                        dstt = qf_t if w_i == 0 else kb_t
                        ve.op("tensor_add", dstt[:, c, :], mn_ch[c % 2][:], r_ch[c % 2][:])
                        ve.mark(f"{tg}_fmc_{w_i}_{c}")
            sy.wait(sy.at("vector", f"{tg}_fmc_0_{KC-1}"))
            sy.op("dma_start", inc=16,
                  out=q_scr.rearrange("(c p) t -> p c t", p=128)[:, :, t_i * TAU:(t_i + 1) * TAU],
                  in_=qf_t[:])
            sy.mark(f"{tg}_qst")
            for s in range(4):
                te.wait(te.at("vector", f"{tg}_fmc_1_{KC-1}"))
                if not (t_i == 0 and s == 0):
                    te.wait(te.at("vector", f"{tg}_kev_{s-1}" if s > 0 else f"at{t_i-1}_kev_3"))
                for c in range(KC):
                    te.op("transpose", pbf[:, c * 128:(c + 1) * 128],
                          kb_t[:, c, s * 128:(s + 1) * 128], idb_t[:],
                          inc=(1 if c == KC - 1 else 0))
                te.mark(f"{tg}_ktr_{s}")
                ve.wait(ve.at("tensor", f"{tg}_ktr_{s}"))
                if not (t_i == 0 and s == 0):
                    ve.wait(ve.at("tensor", f"{tg}_kv_{s-1}" if s > 0 else f"at{t_i-1}_kv_3"))
                ve.op("tensor_copy", k_tm[:], pbf[:, 0:1024])
                ve.mark(f"{tg}_kev_{s}")
                te.wait(te.at("scalar", f"{tg}_zev_2_{KC-1}"))
                if not (t_i == 0 and s == 0):
                    te.wait(te.at("vector", f"{tg}_vev_{s-1}" if s > 0 else f"at{t_i-1}_vev_3"))
                for c in range(KC):
                    te.op("transpose", pbf[:, 1024 + c * 128: 1024 + (c + 1) * 128],
                          vb_t[:, c, s * 128:(s + 1) * 128], idb_t[:],
                          inc=(1 if c == KC - 1 else 0))
                te.mark(f"{tg}_vtr_{s}")
                ve.wait(ve.at("tensor", f"{tg}_vtr_{s}"))
                ve.op("tensor_copy",
                      v_aug[:].rearrange("p (h n) -> p h n", n=65)[:, :, 0:64],
                      pbf[:, 1024:2048].rearrange("p (h n) -> p h n", n=64))
                ve.mark(f"{tg}_vev_{s}")
                te.wait(te.at("vector", f"{tg}_kev_{s}"))
                te.wait(te.at("vector", f"{tg}_vev_{s}"))
                first = (t_i == 0 and s == 0)
                last = (t_i == NATT - 1 and s == 3)
                for h in range(NH):
                    b_i = 4 + h // 8
                    g_i = (h % 8) // 2
                    pr = (h % 2) * 64
                    te.op("matmul", bank(b_i, pr, 64, g_i * 65, 65),
                          k_tm[:, h * 64:(h + 1) * 64],
                          v_aug[:, h * 65:(h + 1) * 65],
                          inc=(1 if h == NH - 1 else 0),
                          start=(first and (h % 8) == 0),
                          stop=(last and (h % 8) == 7),
                          skip_group_check=True)
                te.mark(f"{tg}_kv_{s}")
        for e in ENGINES:
            G[e].mark("aa_end")

        ve.wait(ve.at("tensor", "aa_end"))
        ve.op("tensor_copy", kv_sb[:, 0:260], bank(4, 0, 128, 0, 260))
        ve.op("tensor_copy", kv_sb[:, 260:520], bank(5, 0, 128, 0, 260))
        ve.op("tensor_scalar", cc_sb[:, 0:520], kv_sb[:], bmask_t[:, 0:1], None, ALU.mult)
        ve.op("tensor_scalar", cc_sb[:, 520:1040], kv_sb[:], bmask_t[:, 1:2], None, ALU.mult)
        ve.mark("ab_ccsb")
        gp.wait(gp.at("vector", "ab_ccsb"))
        gp.op("dma_start", inc=16, out=cc_in.ap(), in_=cc_sb[:])
        gp.mark("ab_ccin")
        gp.wait(gp.at("gpsimd", "ab_ccin"))
        gp.op("collective_compute", "AllReduce", ALU.add,
              replica_groups=[[0, 1, 2, 3, 4, 5, 6, 7]],
              ins=[cc_in.ap()], outs=[cc_out.ap()], inc=1)
        gp.mark("ab_cc")
        gp.wait(gp.at("gpsimd", "ab_cc"))
        gp.op("dma_start", inc=16, out=cc_sb[:], in_=cc_out.ap())
        gp.wait(gp.at("tensor", "aa_end"))
        gp.op("dma_start", inc=16, out=wexp_buf[0][:, 0:KC, :],
              in_=wo_d.rearrange("(c p) m -> p c m", p=128))
        gp.mark("ab_rd")
        ve.wait(ve.at("gpsimd", "ab_rd"))
        ve.op("tensor_scalar", kv_my[:], cc_sb[:, 0:520], bmask_t[:, 0:1], None, ALU.mult)
        ve.op("scalar_tensor_tensor", kv_my[:], cc_sb[:, 520:1040], bmask_t[:, 1:2],
              kv_my[:], ALU.mult, ALU.add)
        ve.mark("ab_kvmy")
        sc.wait(sc.at("vector", "ab_kvmy"))
        sc.op("activation", kv_bf[:].rearrange("p a b -> p (a b)"), kv_my[:], AF.Copy)
        sc.mark("ab_kvbf")
        ve.wait(ve.at("scalar", "ab_kvbf"))
        ve.op("memset", den_l[:], 0.0)
        ve.op("memset", kvz[:].rearrange("p a b -> p (a b)"), 0.0)
        for h in range(NH):
            g_i = (h % 8) // 2
            ve.op("tensor_copy", den_l[(h % 2) * 64:(h % 2) * 64 + 64, h:h + 1],
                  kv_bf[(h % 2) * 64:(h % 2) * 64 + 64, h // 8, g_i * 65 + 64: g_i * 65 + 65])
            ve.op("tensor_copy",
                  kvz[(h % 2) * 64:(h % 2) * 64 + 64, h // 2, (h % 2) * 64:(h % 2) * 64 + 64],
                  kv_bf[(h % 2) * 64:(h % 2) * 64 + 64, h // 8, g_i * 65: g_i * 65 + 64])
        ve.mark("ab_denl")
        barrier(G, "ac")

        for t_i in range(NATT):
            tg = f"ac{t_i}"
            if t_i > 0:
                sy.wait(sy.at("tensor", f"ac{t_i-1}_omm_{KC-1}"))
            else:
                sy.wait(sy.at("vector", "ab_denl"))
            sy.op("dma_start", inc=16, out=qf_t[:],
                  in_=q_scr.rearrange("(c p) t -> p c t", p=128)[:, :, t_i * TAU:(t_i + 1) * TAU])
            sy.mark(f"{tg}_qld")
            te.wait(te.at("sync", f"{tg}_qld"))
            for c in range(KC):
                ba = 2 + (c % 2)
                if not (t_i == 0 and c < 2):
                    te.wait(te.at("vector", f"{tg}_asc_{c-2}" if c >= 2 else f"ac{t_i-1}_asc_{KC-2+c}"))
                te.op("matmul", bank(ba), kvz[:, c, :], qf_t[:, c, :],
                      inc=1, start=True, stop=True)
                te.mark(f"{tg}_amm_{c}")
                if c == 0:
                    if t_i > 0:
                        te.wait(te.at("vector", f"ac{t_i-1}_radd_3"))
                else:
                    te.wait(te.at("vector", f"{tg}_zc_{c-1}"))
                te.op("matmul", bank(0, 0, 2, 0, TAU), den_l[:, 2 * c:2 * c + 2], qf_t[:, c, :],
                      inc=1, start=True, stop=True)
                te.mark(f"{tg}_dmm_{c}")
                sc.wait(sc.at("tensor", f"{tg}_dmm_{c}"))
                if c > 0:
                    sc.wait(sc.at("vector", f"{tg}_zc_{c-1}"))
                sc.op("activation", den_sb, bank(0, 0, 2, 0, TAU), AF.Identity, bias=cst_t[0:2, 1:2])
                sc.mark(f"{tg}_den_{c}")
                ve.wait(ve.at("scalar", f"{tg}_den_{c}"))
                ve.op("reciprocal", rc32, den_sb)
                ve.op("tensor_copy", zc_t, rc32)
                ve.mark(f"{tg}_zc_{c}")
                te.wait(te.at("vector", f"{tg}_zc_{c}"))
                if t_i == 0 and c == 0:
                    pass
                elif c == 0:
                    te.wait(te.at("vector", f"ac{t_i-1}_radd_3"))
                if not (t_i == 0 and c == 0):
                    te.wait(te.at("vector", f"{tg}_zbev_{c-1}" if c >= 1 else f"ac{t_i-1}_zbev_{KC-1}"))
                te.op("matmul", bank(1, 0, 128, 0, TAU), sel2_t[:], zc_t,
                      inc=1, start=True, stop=True)
                te.mark(f"{tg}_zb_{c}")
                ve.wait(ve.at("tensor", f"{tg}_zb_{c}"))
                ve.op("tensor_copy", zb_t[c % 2][:], bank(1, 0, 128, 0, TAU))
                ve.mark(f"{tg}_zbev_{c}")
                ve.wait(ve.at("tensor", f"{tg}_amm_{c}"))
                ve.op("tensor_mul", asc_t[:, c, :],
                      bankpair(1)[:, (c % 2) * 512:(c % 2) * 512 + 512], zb_t[c % 2][:])
                ve.mark(f"{tg}_asc_{c}")
            for c in range(KC):
                bo = 4 + (c % 2)
                if c == 0:
                    te.wait(te.at("vector", f"{tg}_asc_{KC-1}"))
                if not (t_i == 0 and c < 2):
                    te.wait(te.at("scalar", f"{tg}_oev_{c-2}" if c >= 2 else f"ac{t_i-1}_oev_{KC-2+c}"))
                for k in range(KC):
                    te.op("matmul", bank(bo), wexp_buf[0][:, k, c * 128:(c + 1) * 128],
                          asc_t[:, k, :],
                          inc=(1 if k == KC - 1 else 0), start=(k == 0), stop=(k == KC - 1))
                te.mark(f"{tg}_omm_{c}")
                sc.wait(sc.at("tensor", f"{tg}_omm_{c}"))
                if c < 2 and t_i > 0:
                    sc.wait(sc.at("tensor", f"ac{t_i-1}_rtr_3"))
                sc.op("activation", of_t[:, c, :], bank(bo), AF.Identity,
                      bias=bias_t[:, 3 * KC + c: 3 * KC + c + 1])
                sc.mark(f"{tg}_oev_{c}")
            for s in range(4):
                ug = t_i * 4 + s
                if s == 0:
                    te.wait(te.at("scalar", f"{tg}_oev_{KC-1}"))
                if not (t_i == 0 and s == 0):
                    te.wait(te.at("vector", f"{tg}_radd_{s-1}" if s > 0 else f"ac{t_i-1}_radd_3"))
                for c in range(KC):
                    te.op("transpose", bankpair(0)[:, c * 128:(c + 1) * 128],
                          of_t[:, c, s * 128:(s + 1) * 128], idf_t[:],
                          inc=(1 if c == KC - 1 else 0))
                te.mark(f"{tg}_rtr_{s}")
                if s >= 2:
                    sy.wait(sy.at("vector", f"{tg}_radd_{s-2}"))
                elif t_i > 0:
                    sy.wait(sy.at("vector", f"ac{t_i-1}_radd_{2+s}"))
                sy.op("dma_start", inc=16, out=xr_tm[s % 2][:], in_=src[ug * 128:(ug + 1) * 128, :])
                sy.mark(f"{tg}_rx_{s}")
                ve.wait(ve.at("tensor", f"{tg}_rtr_{s}"))
                ve.wait(ve.at("sync", f"{tg}_rx_{s}"))
                if not (t_i == 0 and s < 2):
                    ve.wait(ve.at("sync", f"{tg}_rst_{s-2}" if s >= 2 else f"ac{t_i-1}_rst_{2+s}"))
                ve.op("tensor_add", o_tm[s % 2][:], bankpair(0), xr_tm[s % 2][:])
                ve.mark(f"{tg}_radd_{s}")
                sy.wait(sy.at("vector", f"{tg}_radd_{s}"))
                sy.op("dma_start", inc=16, out=dst[ug * 128:(ug + 1) * 128, :], in_=o_tm[s % 2][:])
                sy.mark(f"{tg}_rst_{s}")
        for e in ENGINES:
            G[e].mark("att_end")

    def schedule(G):
        sy, gp, sc, ve, te = (G[n] for n in ENGINES)
        sy.op("dma_start", inc=16, out=idf_t[:], in_=idf_d)
        sy.op("dma_start", inc=16, out=idb_t[:], in_=idb_d)
        sy.op("dma_start", inc=16, out=id32_t[:], in_=id32_d)
        sy.op("dma_start", inc=16, out=sel2_t[:], in_=sel2_d)
        sy.op("dma_start", inc=16,
              out=wrt_t[:],
              in_=wrt_d.rearrange("(c p) e -> p c e", p=128))
        sy.op("dma_start", inc=16, out=ebias_t[:], in_=ebias_d)
        sy.op("dma_start", inc=16, out=bias_t[:], in_=bias_d)
        sy.op("dma_start", inc=16, out=bmask_t[:], in_=bmask_d)
        sy.op("dma_start", inc=16, out=cst_t[:], in_=cst_d)
        sy.mark("prologue_dma")
        for e in ["vector", "scalar", "tensor", "gpsimd"]:
            G[e].wait(G[e].at("sync", "prologue_dma"))
        ve.op("memset", stat[:], 0.0)
        ve.op("memset", comb[32:34, :], 0.5)
        ve.mark("prologue")
        for _ in range(4):
            gp.op("nop")
        gp.mark("prologue_gp")

        if stage in ("ln1", "ln2"):
            cut = int(os.environ.get("KMK_CUT", "99"))
            tag = "f0s0"
            barrier(G, f"{tag}_a")
            for u in range(2):
                emit_ln(G, "f0ln", xin, u, xn_fm, u * 128, stage == "ln2", cut=cut)
                if u == 0:
                    ve.op("tensor_copy", uw_t[0][:, 0:16], stat[:])
                    ve.mark("statsnap")
            sy.wait(sy.at("vector", "statsnap"))
            sy.op("dma_start", inc=16, out=out_d[256:384, 0:16], in_=uw_t[0][:, 0:16])
            ve.wait(ve.at("scalar", "f0ln_evac_1"))
            ve.op("tensor_copy", wbc_t[:, 0:1024].rearrange("p (c t) -> p c t", t=128), xn_fm[:, :, 0:128])
            ve.mark("dump1")
            sy.wait(sy.at("vector", "dump1"))
            sy.op("dma_start", inc=16, out=out_d[0:128, :].rearrange("p (c t) -> p c t", t=128), in_=wbc_t[:, 0:1024].rearrange("p (c t) -> p c t", t=128))
            if stage == "ln2" and int(os.environ.get("KMK_CUT", "99")) >= 99:
                sy.wait(sy.at("vector", "f0ln_combem_1"))
                sy.op("dma_start", inc=16, out=out_d[128:162, 0:256], in_=comb[:, 0:256])
        elif stage == "fa" or stage == "fb1":
            tag = "f0s0"
            barrier(G, f"{tag}_a")
            for u in range(NU):
                emit_ln(G, "f0ln", xin, u, xn_fm, u * 128, True)
            barrier(G, f"{tag}_b")
            if stage == "fb1":
                emit_experts(G, tag, nloop=1)
                barrier(G, f"{tag}_c")
                sy.wait(sy.at("vector", f"{tag}_sweep_end"))
                sy.op("dma_start", inc=16,
                      out=out_d[0:2048, :].rearrange("(r p) m -> p r m", p=128),
                      in_=dacc[:].rearrange("p c t -> p (c t)").rearrange(
                          "p (r m) -> p r m", m=1024))
            else:
                sy.wait(sy.at("vector", f"f0ln_combem_{NU-1}"))
                sy.op("dma_start", inc=16, out=out_d[0:34, 0:1024], in_=comb[:, 0:1024])
                sy.op("dma_start", inc=16, out=out_d[34:68, 0:1024], in_=comb[:, 1024:2048])
        elif stage == "f1":
            emit_f_call(G, 0, xin, out_d)
        else:
            emit_f_call(G, 0, xin, x2_scr)
            if stage == "a":
                emit_attn(G, x2_scr, out_d)
            elif stage == "adbg":
                emit_attn(G, x2_scr, x3_scr)
                ve.wait(ve.at("vector", "att_end"))
                ve.op("tensor_copy", wbc_t[:, 0:520], kv_my[:, :])
                ve.mark("dbg1")
                sy.wait(sy.at("vector", "dbg1"))
                sy.op("dma_start", inc=16, out=out_d[0:128, 0:520], in_=wbc_t[:, 0:520])
                sy.op("dma_start", inc=16, out=qf_t[:],
                      in_=q_scr.rearrange("(c p) t -> p c t", p=128)[:, :, 0:TAU])
                sy.mark("dbgq")
                ve.wait(ve.at("sync", "dbgq"))
                ve.op("tensor_copy", wbc_t[:].rearrange("p (c t) -> p c t", t=TAU),
                      qf_t[:, 0:4, :])
                ve.mark("dbg2")
                sy.wait(sy.at("vector", "dbg2"))
                sy.op("dma_start", inc=16,
                      out=out_d[1024:1280, :].rearrange("(c2 p) m -> p c2 m", p=128),
                      in_=wbc_t[:].rearrange("p (c2 m) -> p c2 m", m=1024))
                sy.mark("dbg2x")
                ve.wait(ve.at("vector", "dbg2"))
                ve.wait(ve.at("sync", "dbg2x"))
                ve.op("tensor_copy", o_tm[0][:, 0:16], den_l[:, :])
                ve.mark("dbg3")
                sy.wait(sy.at("vector", "dbg3"))
                sy.op("dma_start", inc=16, out=out_d[256:384, 0:16], in_=o_tm[0][:, 0:16])
                ve.op("tensor_copy", o_tm[1][:].rearrange("p (c t) -> p c t", t=TAU)[:, 0:2, :],
                      asc_t[:, 0:2, :])
                ve.mark("dbg4")
                sy.wait(sy.at("vector", "dbg4"))
                sy.op("dma_start", inc=16, out=out_d[384:512, :], in_=o_tm[1][:, :])
                ve.op("tensor_copy", o_tm[0][:, 16:18].rearrange("p a -> a p") if False else o_tm[0][0:2, 16:528], den_sb)
                ve.mark("dbg5")
                sy.wait(sy.at("vector", "dbg5"))
                sy.op("dma_start", inc=16, out=out_d[1280:1282, 0:512], in_=o_tm[0][0:2, 16:528])
                sy.op("dma_start", inc=16, out=out_d[1284:1412, 0:512], in_=zb_t[1][:, :])
                sy.op("dma_start", inc=16, out=out_d[512:1024, :], in_=x3_scr[0:512, :])
            else:
                emit_attn(G, x2_scr, x3_scr)
                emit_f_call(G, 1, x3_scr, out_d)

        for e in ENGINES:
            G[e].mark("the_end")
        if sy.emit:
            sy.eng.wait_ge(sems["sync"], plan[("sync", "the_end")].a)

    G0 = {n: Eng(n, plan, False, record=True) for n in ENGINES}
    schedule(G0)
    totals = {n: G0[n].cnt for n in ENGINES}

    with nc.Block() as block:
        def mk_section(name):
            def section(eng):
                G = {n: Eng(n, plan, n == name, eng if n == name else None, sems)
                     for n in ENGINES}
                schedule(G)
                assert G[name].cnt == totals[name], f"{name}: {G[name].cnt} != {totals[name]}"
            return section

        block.sync(mk_section("sync"))
        block.gpsimd(mk_section("gpsimd"))
        block.scalar(mk_section("scalar"))
        block.vector(mk_section("vector"))
        block.tensor(mk_section("tensor"))

    return nc, totals


def _prep_inputs(inp):
    import ml_dtypes
    f = np.float32
    gupT = np.transpose(np.asarray(inp["L0_ffn_gate_up_proj"], f), (0, 2, 1))
    sg = np.asarray(inp["L0_ffn_shared_gate_w"], f).T
    su = np.asarray(inp["L0_ffn_shared_up_w"], f).T
    sd = np.asarray(inp["L0_ffn_shared_down_w"], f).T
    ps1 = np.concatenate([sg[:, 0:512], su[:, 0:512]], axis=1)[None]
    ps2 = np.concatenate([sg[:, 512:1024], su[:, 512:1024]], axis=1)[None]
    gup_all = np.ascontiguousarray(np.concatenate([gupT, ps1, ps2], axis=0))
    dwnT = np.transpose(np.asarray(inp["L0_ffn_down_proj"], f), (0, 2, 1))
    dwn_all = np.ascontiguousarray(
        np.concatenate([dwnT, sd[None, 0:512, :], sd[None, 512:1024, :]], axis=0))

    def lhsT(w):
        return np.ascontiguousarray(np.asarray(w, f).T)

    def bias_col(b):
        return np.ascontiguousarray(np.asarray(b, f).reshape(KC, 128).T)

    biases = np.concatenate([bias_col(inp["L0_attn_bq"]), bias_col(inp["L0_attn_bk"]),
                             bias_col(inp["L0_attn_bv"]), bias_col(inp["L0_attn_bo"])],
                            axis=1)

    wexp = np.ascontiguousarray(np.concatenate([gup_all, dwn_all], axis=1))
    common = {
        "wexp": wexp,
        "wq": lhsT(inp["L0_attn_Wq"]), "wk": lhsT(inp["L0_attn_Wk"]),
        "wv": lhsT(inp["L0_attn_Wv"]), "wo": lhsT(inp["L0_attn_Wo"]),
        "wrt": lhsT(inp["L0_ffn_router_weight"]),
        "bqkvo": np.ascontiguousarray(biases),
        "ebias": np.ascontiguousarray(
            np.broadcast_to(np.asarray(inp["L0_ffn_e_score_bias"], f), (128, 32))),
        "idf": np.eye(128, dtype=f),
        "idb": np.eye(128).astype(ml_dtypes.bfloat16),
        "id32": np.eye(32, dtype=f),
        "sel2": np.repeat(np.eye(2, dtype=f), 64, axis=1),
        "cst": np.ascontiguousarray(np.broadcast_to(
            np.array([EPS, 1e-6, -0.5, 0.0], f), (128, 4))),
    }
    x = np.asarray(inp["L0_attn_hidden_states"], dtype=np.float32)
    in_maps = []
    for c in range(8):
        b, s0 = c // 4, (c % 4) * TC
        m = dict(common)
        m["xin"] = np.ascontiguousarray(x[b, s0:s0 + TC, :])
        bm = np.zeros((128, 2), f)
        bm[:, b] = 1.0
        m["bmask"] = bm
        in_maps.append(m)
    return in_maps


_CACHE = {}


def kernel(**inputs):
    stage = os.environ.get("KMK_STAGE", "full")
    in_maps = _prep_inputs(inputs)
    if stage not in _CACHE:
        _CACHE[stage] = build_program(stage)
    nc, _ = _CACHE[stage]
    res = run_bass_kernel_spmd(nc, in_maps, core_ids=list(range(8)))
    full = np.empty((2, 16384, D), np.float32)
    for c in range(8):
        full[c // 4, (c % 4) * TC:(c % 4 + 1) * TC, :] = res.results[c]["out"]
    return full


# version 06
# speedup vs baseline: 1.1382x; 1.0033x over previous
import os
import numpy as np
from contextlib import ExitStack

import concourse.bass as bass
import concourse.mybir as mybir
from concourse.bass_utils import run_bass_kernel_spmd

F32 = mybir.dt.float32
F32R = mybir.dt.float32r
BF16 = mybir.dt.bfloat16
AF = mybir.ActivationFunctionType
ALU = mybir.AluOpType
AX = mybir.AxisListType.X

D = 1024
KC = 8
TC = 4096
ST = 2048
NST = TC // ST
TAU = 512
NTAU = ST // TAU
NU = ST // 128
NE = 34
NH = 16
EPS = 1e-5
NATT = TC // TAU
NLOOP = NE // 2


class Aff:
    __slots__ = ("a", "b")

    def __init__(self, a, b=0):
        self.a, self.b = a, b


class Eng:
    _uid = [0]

    def __init__(self, name, plan, emit, eng=None, sems=None, record=False):
        self.name = name
        self.plan = plan
        self.emit = emit
        self.record = record
        self.eng = eng
        self.sems = sems
        self.cnt = 0
        self.coef = 0
        self.loop_var = None
        self.in_loop = False
        self._loop_marks = []
        self._dma_id = 0
        self._ss_id = 0

    def mark(self, key):
        k = (self.name, key)
        a = self.cnt + self.coef if self.in_loop else self.cnt
        if self.record:
            if self.in_loop:
                m = Aff(a, -1)
                self.plan[k] = m
                self._loop_marks.append(k)
            else:
                self.plan[k] = Aff(a, 0)
        else:
            p = self.plan[k]
            assert p.a == a, f"drift {k}: plan={p.a} now={a}"

    def at(self, eng_name, key, delta=0, prev_iter=False):
        return (eng_name, key, delta, prev_iter)

    def wait(self, target):
        if not self.emit:
            return
        eng_name, key, delta, prev_iter = target
        aff = self.plan[(eng_name, key)]
        a, b = aff.a + delta, aff.b
        assert b >= 0, f"unpatched milestone {(eng_name, key)}"
        if prev_iter:
            a -= b
        if b == 0 or not self.in_loop:
            assert b == 0, f"affine milestone {(eng_name, key)} used outside loop"
            if a <= 0:
                return
            self.eng.wait_ge(self.sems[eng_name], a)
        else:
            assert self.loop_var is not None
            assert a >= 0, f"negative wait base {a} for {(eng_name, key)}"
            if b not in self._rcoef:
                Eng._uid[0] += 1
                r = self.eng.alloc_register(f"rc_{self.name}_{b}_{Eng._uid[0]}")
                self.eng.reg_alu(r, self.loop_var, b, op=ALU.mult)
                self._rcoef[b] = r
            self.eng.reg_alu(self._rtmp, self._rcoef[b], a, op=ALU.add)
            self.eng.wait_ge(self.sems[eng_name], self._rtmp)

    def op(self, fn_name, *args, inc=1, **kwargs):
        if fn_name in ("dma_start", "collective_compute") or                 self.name in ("vector", "scalar", "gpsimd"):
            self._dma_id += 1
            key = f"__dma_{self.name}_{self._dma_id}"
            self.mark(key)
            self.wait(self.at(self.name, key))
        if self.emit:
            ins = getattr(self.eng, fn_name)(*args, **kwargs)
            if inc:
                ins.then_inc(self.sems[self.name], inc)
        if inc:
            if self.in_loop:
                self.coef += inc
            else:
                self.cnt += inc

    def selfsync(self):
        self._ss_id += 1
        key = f"__ss_{self.name}_{self._ss_id}"
        self.mark(key)
        self.wait(self.at(self.name, key))

    def loop(self, trip):
        outer = self

        class L:
            def __enter__(self_l):
                outer.in_loop = True
                outer.coef = 0
                outer._loop_marks = []
                outer._rcoef = {}
                if outer.emit:
                    self_l._cm = outer.eng.Fori(0, trip)
                    outer.loop_var = self_l._cm.__enter__()
                    Eng._uid[0] += 1
                    outer._rtmp = outer.eng.alloc_register(f"rt_{outer.name}_{Eng._uid[0]}")
                return outer.loop_var

            def __exit__(self_l, *exc):
                if outer.emit:
                    for r in outer._rcoef.values():
                        outer.eng.free_register(r)
                    outer.eng.free_register(outer._rtmp)
                    self_l._cm.__exit__(*exc)
                if outer.record:
                    for k in outer._loop_marks:
                        outer.plan[k].b = outer.coef
                outer.cnt += outer.coef * trip
                outer.coef = 0
                outer.in_loop = False
                outer.loop_var = None
                return False
        return L()


ENGINES = ["sync", "gpsimd", "scalar", "vector", "tensor"]


def build_program(stage="full"):
    nc = bass.Bass("TRN2", target_bir_lowering=False)

    xin = nc.dram_tensor("xin", [TC, D], F32, kind="ExternalInput").ap()
    wexp_d = nc.dram_tensor("wexp", [NE, 1536, D], F32, kind="ExternalInput").ap()
    wq_d = nc.dram_tensor("wq", [D, D], F32, kind="ExternalInput").ap()
    wk_d = nc.dram_tensor("wk", [D, D], F32, kind="ExternalInput").ap()
    wv_d = nc.dram_tensor("wv", [D, D], F32, kind="ExternalInput").ap()
    wo_d = nc.dram_tensor("wo", [D, D], F32, kind="ExternalInput").ap()
    wrt_d = nc.dram_tensor("wrt", [D, 32], F32R, kind="ExternalInput").ap()
    bias_d = nc.dram_tensor("bqkvo", [128, 4 * KC], F32, kind="ExternalInput").ap()
    ebias_d = nc.dram_tensor("ebias", [128, 32], F32, kind="ExternalInput").ap()
    idf_d = nc.dram_tensor("idf", [128, 128], F32, kind="ExternalInput").ap()
    idb_d = nc.dram_tensor("idb", [128, 128], BF16, kind="ExternalInput").ap()
    id32_d = nc.dram_tensor("id32", [32, 32], F32, kind="ExternalInput").ap()
    sel2_d = nc.dram_tensor("sel2", [2, 128], F32R, kind="ExternalInput").ap()
    bmask_d = nc.dram_tensor("bmask", [128, 2], F32, kind="ExternalInput").ap()
    cst_d = nc.dram_tensor("cst", [128, 4], F32, kind="ExternalInput").ap()

    out_d = nc.dram_tensor("out", [TC, D], F32, kind="ExternalOutput").ap()
    x2_scr = nc.dram_tensor("x2_scr", [TC, D], F32).ap()
    x3_scr = nc.dram_tensor("x3_scr", [TC, D], F32).ap()
    q_scr = nc.dram_tensor("q_scr", [D, TC], BF16).ap()
    comb_dram = nc.dram_tensor("comb_dram", [NE, ST], F32).ap()
    cc_in = nc.dram_tensor("cc_in", [128, 4 * 260], F32)
    cc_out = nc.dram_tensor("cc_out", [128, 4 * 260], F32)

    ctx = ExitStack()

    def sbt(name, shape, dtype):
        return ctx.enter_context(nc.sbuf_tensor(name, shape, dtype))

    idf_t = sbt("idf_t", [128, 128], F32)
    id32_t = sbt("id32_t", [32, 32], F32)
    idb_t = sbt("idb_t", [128, 128], BF16)
    sel2_t = sbt("sel2_t", [2, 128], F32R)
    wrt_t = sbt("wrt_t", [128, KC, 32], F32R)
    ebias_t = sbt("ebias_t", [128, 32], F32)
    bias_t = sbt("bias_t", [128, 4 * KC], F32)
    bmask_t = sbt("bmask_t", [128, 2], F32)
    cst_t = sbt("cst_t", [128, 4], F32)

    wexp_buf = [sbt(f"wexp{i}", [128, 12, D], BF16) for i in range(2)]

    xn_fm = sbt("xn_fm", [128, KC, ST], BF16)
    dacc = sbt("dacc", [128, KC, ST], F32)
    comb = sbt("comb", [NE, ST], F32)
    wbc_t = sbt("wbc", [128, ST], F32)
    wbc = [wbc_t, wbc_t]
    x_tm = [sbt(f"x_tm{i}", [128, D], F32) for i in range(2)]
    xr_tm = x_tm
    o_tm = [sbt(f"o_tm{i}", [128, D], F32) for i in range(2)]
    xn_tm = sbt("xn_tm", [128, D], F32)
    xn32 = sbt("xn32", [128, KC, 128], F32R)
    stat = sbt("stat", [128, 16], F32)
    lg_s = sbt("lg_s", [32, 128], F32)
    lg_tm = sbt("lg_tm", [128, 32], F32)
    ex_t = sbt("ex_t", [128, 32], F32)
    rw_t = sbt("rw_t", [128, 32], F32)
    corr_t = sbt("corr_t", [128, 32], F32)
    m8_t = sbt("m8_t", [128, 8], F32)
    mask_t = sbt("mask_t", [128, 32], F32)
    comb_tm = sbt("comb_tm", [128, 32], F32)

    sg32 = [sbt(f"sg32_{i}", [128, TAU], F32) for i in range(2)]
    uw_t = [sbt(f"uw_{i}", [128, TAU], F32) for i in range(2)]
    hmid = sbt("hmid", [128, 4, TAU], BF16)

    def carve(base, byte_off, shape, dtype):
        tot = 1
        for d in base.shape[1:]:
            tot *= d
        h = base.reshape([128, tot]).bitcast(dtype)
        esz = {F32: 4, F32R: 4, BF16: 2}[dtype]
        n = 1
        for d in shape[1:]:
            n *= d
        assert byte_off % esz == 0
        e0 = byte_off // esz
        ap = h[:, e0:e0 + n]
        if len(shape) == 3:
            ap = ap.rearrange("p (a b) -> p a b", b=shape[2])
        return ap

    of_t = carve(dacc, 0, [128, KC, TAU], F32)
    asc_t = carve(dacc, 16384, [128, KC, TAU], BF16)
    qf_t = carve(dacc, 24576, [128, KC, TAU], BF16)
    kb_t = carve(dacc, 32768, [128, KC, TAU], BF16)
    vb_t = carve(dacc, 40960, [128, KC, TAU], BF16)
    xn_a = carve(dacc, 49152, [128, KC, TAU], BF16)
    cc_sb = carve(dacc, 57344, [128, 4 * 260], F32)
    z_ch = [carve(xn_fm, 0 + i * 2048, [128, TAU], F32) for i in range(2)]
    r_ch = [carve(xn_fm, 4096 + i * 2048, [128, TAU], F32) for i in range(2)]
    mn_ch = [carve(xn_fm, 8192 + i * 2048, [128, TAU], F32) for i in range(2)]

    kv_sb = carve(xn_fm, 20480, [128, 2 * 260], F32)
    kv_my = carve(xn_fm, 22560, [128, 2 * 260], F32)
    k_tm = carve(xn_fm, 24640, [128, D], BF16)
    v_aug = carve(xn_fm, 26688, [128, NH * 65], BF16)
    kv_bf = carve(xn_fm, 28768, [128, 2, 260], BF16)
    kvz = carve(dacc, 61504, [128, KC, 128], BF16)
    den_l = carve(xn_fm, 29808, [128, 16], BF16)
    zb_t = [carve(xn_fm, 12288 + i * 2048, [128, TAU], F32) for i in range(2)]
    den_sb = carve(xn_fm, 16384, [128, TAU], F32)[0:2, :]
    zc_full = sbt("zc_t", [2, TAU], F32R)
    zc_t = zc_full[0:2, :]
    rc32 = carve(xn_fm, 29952, [128, TAU], F32)[0:2, :]

    pctx = ExitStack()
    pbl = [pctx.enter_context(nc.psum_tensor(f"pbl{i}", [128, 1024], F32)) for i in range(3)]
    pbf = pctx.enter_context(nc.psum_tensor("pbf", [128, 2048], BF16))

    sem_ctx = ExitStack()
    sems = {n: sem_ctx.enter_context(nc.semaphore(f"s_{n}")) for n in ENGINES}

    plan = {}

    def bank(i, p0=0, np_=128, c0=0, nc_=512):
        t = pbl[i // 2]
        off = (i % 2) * 512
        return t[p0:p0 + np_, off + c0: off + c0 + nc_]

    def bankpair(j):
        return pbl[j][:, :]

    def barrier(G, name, waiters=("tensor", "vector", "scalar", "sync")):
        markers = ENGINES
        for m in markers:
            G[m].mark(f"bar_{name}")
        for w in waiters:
            for m in markers:
                if m != w:
                    G[w].wait(G[w].at(m, f"bar_{name}"))

    def emit_ln(G, tag, x_src, ug, dst_fm, dst_off, want_router, st_off=0, force_j=None, cut=99):
        sy, gp, sc, ve, te = (G[n] for n in ENGINES)
        xb = x_tm[ug % 2]

        if ug >= 2:
            sy.wait(sy.at("vector", f"{tag}_xn_{ug-2}"))
            sy.wait(sy.at("scalar", f"{tag}_sq_{ug-2}"))
        sy.op("dma_start", inc=16, out=xb[:], in_=x_src[ug * 128:(ug + 1) * 128, :])
        sy.mark(f"{tag}_x_{ug}")

        ve.wait(ve.at("sync", f"{tag}_x_{ug}"))
        ve.op("reduce_sum", stat[:, 0:1], xb[:], axis=AX, negate=True)
        sc.wait(sc.at("sync", f"{tag}_x_{ug}"))
        if ug >= 1:
            sc.wait(sc.at("vector", f"{tag}_var_{ug-1}"))
            sc.wait(sc.at("tensor", f"{tag}_tr_{ug-1}"))
        sc.op("activation", xn_tm[:], xb[:], AF.Square)
        sc.mark(f"{tag}_sq_{ug}")
        ve.wait(ve.at("scalar", f"{tag}_sq_{ug}"))
        ve.op("reduce_sum", stat[:, 1:2], xn_tm[:], axis=AX)
        ve.selfsync()
        ve.op("tensor_scalar", stat[:, 2:3], stat[:, 0:1], 1.0 / D, None, ALU.mult)
        ve.mark(f"{tag}_s1_{ug}")
        ve.wait(ve.at("vector", f"{tag}_s1_{ug}"))
        ve.op("tensor_mul", stat[:, 10:11], stat[:, 2:3], stat[:, 2:3])
        ve.mark(f"{tag}_msq_{ug}")
        ve.wait(ve.at("vector", f"{tag}_msq_{ug}"))
        ve.op("scalar_tensor_tensor", stat[:, 3:4], stat[:, 1:2], 1.0 / D, stat[:, 10:11],
              ALU.mult, ALU.subtract)
        ve.mark(f"{tag}_var_{ug}")
        sc.wait(sc.at("vector", f"{tag}_var_{ug}"))
        sc.op("activation", stat[:, 11:12], stat[:, 3:4], AF.Ln, bias=cst_t[:, 0:1])
        sc.selfsync()
        sc.op("activation", stat[:, 4:5], stat[:, 11:12], AF.Exp, scale=cst_t[:, 2:3])
        sc.mark(f"{tag}_rstd_{ug}")
        ve.wait(ve.at("scalar", f"{tag}_rstd_{ug}"))
        ve.op("tensor_scalar", xn_tm[:], xb[:], stat[:, 2:3], stat[:, 4:5], ALU.add, ALU.mult)
        ve.mark(f"{tag}_xn_{ug}")

        j = (ug % 2) if force_j is None else force_j
        back = 2 if force_j is None else 1
        te.wait(te.at("vector", f"{tag}_xn_{ug}"))
        if ug >= back:
            te.wait(te.at("scalar", f"{tag}_evac_{ug-back}"))
            if want_router:
                te.wait(te.at("vector", f"{tag}_evac32_{ug-back}"))
        for c in range(KC):
            te.op("transpose", bankpair(j)[:, c * 128:(c + 1) * 128],
                  xn_tm[:, c * 128:(c + 1) * 128], idf_t[:],
                  inc=(1 if c == KC - 1 else 0))
        te.mark(f"{tag}_tr_{ug}")

        sc.wait(sc.at("tensor", f"{tag}_tr_{ug}"))
        sc.op("activation",
              dst_fm[:, :, dst_off:dst_off + 128],
              bankpair(j).rearrange("p (c t) -> p c t", t=128), AF.Copy)
        sc.mark(f"{tag}_evac_{ug}")
        if not want_router:
            return

        ve.wait(ve.at("tensor", f"{tag}_tr_{ug}"))
        ve.wait(ve.at("scalar", f"{tag}_evac_{ug}"))
        ve.op("tensor_copy", xn32[:].rearrange("p c t -> p (c t)"), bankpair(j))
        ve.mark(f"{tag}_evac32_{ug}")
        if cut <= 0:
            return

        te.wait(te.at("vector", f"{tag}_evac32_{ug}"))
        if ug >= 1:
            te.wait(te.at("vector", f"{tag}_lgs_{ug-1}"))
        for c in range(KC):
            te.op("matmul", bank(4, 0, 32, 0, 128), wrt_t[:, c, :], xn32[:, c, :],
                  inc=(1 if c == KC - 1 else 0), start=(c == 0), stop=(c == KC - 1))
        te.mark(f"{tag}_rmm_{ug}")
        ve.wait(ve.at("tensor", f"{tag}_rmm_{ug}"))
        ve.op("tensor_copy", lg_s[:], bank(4, 0, 32, 0, 128))
        ve.mark(f"{tag}_lgs_{ug}")
        if cut <= 1:
            return
        te.wait(te.at("vector", f"{tag}_lgs_{ug}"))
        if ug >= 1:
            te.wait(te.at("vector", f"{tag}_combem_{ug-1}"))
        te.op("transpose", bank(5, 0, 128, 0, 32), lg_s[:], id32_t[:], inc=1)
        te.mark(f"{tag}_lgtr_{ug}")
        ve.wait(ve.at("tensor", f"{tag}_lgtr_{ug}"))
        ve.op("tensor_copy", lg_tm[:], bank(5, 0, 128, 0, 32))
        if cut <= 2:
            return
        ve.op("reduce_max", stat[:, 5:6], lg_tm[:], axis=AX, negate=True)
        ve.mark(f"{tag}_nmax_{ug}")
        sc.wait(sc.at("vector", f"{tag}_nmax_{ug}"))
        sc.op("activation", ex_t[:], lg_tm[:], AF.Exp, bias=stat[:, 5:6])
        sc.mark(f"{tag}_exp_{ug}")
        ve.wait(ve.at("scalar", f"{tag}_exp_{ug}"))
        ve.op("reduce_sum", stat[:, 6:7], ex_t[:], axis=AX)
        ve.mark(f"{tag}_rsum_{ug}")
        ve.wait(ve.at("vector", f"{tag}_rsum_{ug}"))
        ve.op("reciprocal", stat[:, 7:8], stat[:, 6:7])
        ve.mark(f"{tag}_rinv_{ug}")
        if cut <= 3:
            return
        ve.wait(ve.at("vector", f"{tag}_rinv_{ug}"))
        ve.op("tensor_scalar", rw_t[:], ex_t[:], stat[:, 7:8], None, ALU.mult)
        ve.op("tensor_add", corr_t[:], rw_t[:], ebias_t[:])
        ve.op("max", out=m8_t[:], in_=corr_t[:])
        ve.mark(f"{tag}_m8_{ug}")
        if cut <= 4:
            return
        ve.wait(ve.at("vector", f"{tag}_m8_{ug}"))
        ve.op("tensor_scalar", mask_t[:], corr_t[:], m8_t[:, 5:6], None, ALU.is_ge)
        ve.op("tensor_mul", rw_t[:], rw_t[:], mask_t[:])
        ve.op("reduce_sum", stat[:, 8:9], rw_t[:], axis=AX)
        ve.mark(f"{tag}_wsum_{ug}")
        ve.wait(ve.at("vector", f"{tag}_wsum_{ug}"))
        ve.op("tensor_scalar", stat[:, 12:13], stat[:, 8:9], 1e-12, None, ALU.max)
        ve.mark(f"{tag}_wclip_{ug}")
        ve.wait(ve.at("vector", f"{tag}_wclip_{ug}"))
        ve.op("reciprocal", stat[:, 9:10], stat[:, 12:13])
        ve.mark(f"{tag}_winv_{ug}")
        ve.wait(ve.at("vector", f"{tag}_winv_{ug}"))
        ve.op("tensor_scalar", comb_tm[:], rw_t[:], stat[:, 9:10], 0.5, ALU.mult, ALU.mult)
        ve.mark(f"{tag}_comb_{ug}")
        te.wait(te.at("vector", f"{tag}_comb_{ug}"))
        te.op("transpose", bank(5, 0, 32, 0, 128), comb_tm[:], idf_t[:], inc=1)
        te.mark(f"{tag}_combtr_{ug}")
        ve.wait(ve.at("tensor", f"{tag}_combtr_{ug}"))
        ve.op("tensor_copy", comb[0:32, st_off + dst_off: st_off + dst_off + 128],
              bank(5, 0, 32, 0, 128))
        ve.mark(f"{tag}_combem_{ug}")

    def emit_experts(G, tag, nloop=NLOOP):
        sy, gp, sc, ve, te = (G[n] for n in ENGINES)

        ve.op("memset", dacc[:].rearrange("p c t -> p (c t)"), 0.0)
        ve.mark(f"{tag}_dacc0")
        te.wait(te.at("vector", f"{tag}_dacc0"))

        def prev_ref(par, t_i, idx, n_idx):
            flat = (par * NTAU + t_i) * n_idx + idx
            return flat

        def expert_body(par):
            for t_i in range(NTAU):
                for p in range(4):
                    bg, bu = 2 * (p % 2), 2 * (p % 2) + 1
                    pp, ppar, pti = p - 2, par, t_i
                    if pp < 0:
                        pp += 4
                        pti -= 1
                        if pti < 0:
                            pti += NTAU
                            ppar = 1 - ppar
                    prev = (par == 0 and t_i == 0 and p < 2)
                    te.wait(te.at("scalar", f"{tag}_silu_{ppar}_{pti}_{pp}", prev_iter=prev))
                    te.wait(te.at("vector", f"{tag}_uw_{ppar}_{pti}_{pp}", prev_iter=prev))
                    for k in range(KC):
                        te.op("matmul", bank(bg), wexp_buf[par][:, k, p * 128:(p + 1) * 128],
                              xn_fm[:, k, t_i * TAU:(t_i + 1) * TAU],
                              inc=0, start=(k == 0), stop=(k == KC - 1))
                    for k in range(KC):
                        te.op("matmul", bank(bu),
                              wexp_buf[par][:, k, 512 + p * 128: 512 + (p + 1) * 128],
                              xn_fm[:, k, t_i * TAU:(t_i + 1) * TAU],
                              inc=(1 if k == KC - 1 else 0), start=(k == 0), stop=(k == KC - 1))
                    te.mark(f"{tag}_pair_{par}_{t_i}_{p}")
                    sc.wait(sc.at("tensor", f"{tag}_pair_{par}_{t_i}_{p}"))
                    sc.wait(sc.at("vector", f"{tag}_hm_{ppar}_{pti}_{pp}", prev_iter=prev))
                    sc.op("activation", sg32[p % 2][:], bank(bg), AF.Silu)
                    sc.mark(f"{tag}_silu_{par}_{t_i}_{p}")
                    ve.wait(ve.at("tensor", f"{tag}_pair_{par}_{t_i}_{p}"))
                    ve.wait(ve.at("sync", f"{tag}_wbc_{par}"))
                    ve.op("tensor_mul", uw_t[p % 2][:], bank(bu), wbc[par][:, t_i * TAU:(t_i + 1) * TAU])
                    ve.mark(f"{tag}_uw_{par}_{t_i}_{p}")
                    ve.wait(ve.at("scalar", f"{tag}_silu_{par}_{t_i}_{p}"))
                    ve.op("tensor_mul", hmid[:, p, :], sg32[p % 2][:], uw_t[p % 2][:])
                    ve.mark(f"{tag}_hm_{par}_{t_i}_{p}")
                for h in range(KC):
                    bd = 4 + (h % 2)
                    hh, hpar, hti = h - 2, par, t_i
                    if hh < 0:
                        hh += KC
                        hti -= 1
                        if hti < 0:
                            hti += NTAU
                            hpar = 1 - hpar
                    prev = (par == 0 and t_i == 0 and h < 2)
                    te.wait(te.at("vector", f"{tag}_dn_{hpar}_{hti}_{hh}", prev_iter=prev))
                    for k in range(4):
                        if h == 0:
                            te.wait(te.at("vector", f"{tag}_hm_{par}_{t_i}_{k}"))
                        te.op("matmul", bank(bd), wexp_buf[par][:, 8 + k, h * 128:(h + 1) * 128],
                              hmid[:, k, :],
                              inc=(1 if k == 3 else 0), start=(k == 0), stop=(k == 3))
                    te.mark(f"{tag}_dnmm_{par}_{t_i}_{h}")
                    ve.wait(ve.at("tensor", f"{tag}_dnmm_{par}_{t_i}_{h}"))
                    ve.op("scalar_tensor_tensor",
                          dacc[:, h, t_i * TAU:(t_i + 1) * TAU],
                          bank(bd), 1.0, dacc[:, h, t_i * TAU:(t_i + 1) * TAU],
                          ALU.mult, ALU.add)
                    ve.mark(f"{tag}_dn_{par}_{t_i}_{h}")

        sy.op("dma_start", inc=16, out=comb_dram, in_=comb[:])
        sy.mark(f"{tag}_combd")
        with sy.loop(nloop), gp.loop(nloop), sc.loop(nloop), ve.loop(nloop), te.loop(nloop):
            for par in range(2):
                gp.wait(gp.at("tensor", f"{tag}_dnmm_{par}_3_7", prev_iter=True))
                if gp.emit:
                    e_expr = gp.loop_var * 2 + par
                    gp.op("dma_start", inc=16,
                          out=wexp_buf[par][:],
                          in_=wexp_d[e_expr].rearrange("(c p) m -> p c m", p=128))
                else:
                    gp.op("dma_start", inc=16)
                gp.mark(f"{tag}_wld_{par}")
            for par in range(2):
                ppar_w = 1 - par
                prev_w = (par == 0)
                sy.wait(sy.at("vector", f"{tag}_uw_{ppar_w}_3_3", prev_iter=prev_w))
                if sy.emit:
                    e_expr = sy.loop_var * 2 + par
                    sy.op("dma_start", inc=16, out=wbc[par][:],
                          in_=comb_dram[bass.ts(e_expr, 1), :].broadcast_to([128, ST]))
                else:
                    sy.op("dma_start", inc=16)
                sy.mark(f"{tag}_wbc_{par}")
            for par in range(2):
                te.wait(te.at("gpsimd", f"{tag}_wld_{par}"))
                expert_body(par)
        for e in ENGINES:
            G[e].mark(f"{tag}_sweep_end")

    def emit_resid(G, tag, x_src, dst, st):
        sy, gp, sc, ve, te = (G[n] for n in ENGINES)
        for u in range(NU):
            ug = st * NU + u
            j = u % 2
            if u == 0:
                te.wait(te.at("vector", f"{tag}_sweep_end"))
                te.wait(te.at("scalar", f"{tag}_sweep_end"))
                te.wait(te.at("gpsimd", f"{tag}_sweep_end"))
            if u >= 2:
                te.wait(te.at("vector", f"{tag}_radd_{u-2}"))
            for c in range(KC):
                te.op("transpose", bankpair(j)[:, c * 128:(c + 1) * 128],
                      dacc[:, c, u * 128:(u + 1) * 128], idf_t[:],
                      inc=(1 if c == KC - 1 else 0))
            te.mark(f"{tag}_rtr_{u}")
            if u >= 2:
                sc.wait(sc.at("vector", f"{tag}_radd_{u-2}"))
            sc.op("dma_start", inc=16, out=xr_tm[j][:], in_=x_src[ug * 128:(ug + 1) * 128, :])
            sc.mark(f"{tag}_rx_{u}")
            ve.wait(ve.at("tensor", f"{tag}_rtr_{u}"))
            ve.wait(ve.at("scalar", f"{tag}_rx_{u}"))
            if u >= 2:
                ve.wait(ve.at("sync", f"{tag}_rst_{u-2}"))
            ve.op("tensor_add", o_tm[j][:], bankpair(j), xr_tm[j][:])
            ve.mark(f"{tag}_radd_{u}")
            sy.wait(sy.at("vector", f"{tag}_radd_{u}"))
            sy.op("dma_start", inc=16, out=dst[ug * 128:(ug + 1) * 128, :], in_=o_tm[j][:])
            sy.mark(f"{tag}_rst_{u}")

    def emit_f_call(G, cid, x_src, dst):
        for st in range(NST):
            tag = f"f{cid}s{st}"
            barrier(G, f"{tag}_a")
            for u in range(NU):
                emit_ln(G, f"f{cid}ln", x_src, st * NU + u, xn_fm, u * 128, True)
            barrier(G, f"{tag}_b")
            emit_experts(G, tag)
            barrier(G, f"{tag}_c")
            emit_resid(G, tag, x_src, dst, st)

    def emit_attn(G, src, dst):
        sy, gp, sc, ve, te = (G[n] for n in ENGINES)
        barrier(G, "att_w")
        gp.wait(gp.at("tensor", "bar_att_w"))
        gp.op("dma_start", inc=16, out=wexp_buf[0][:, 0:KC, :],
              in_=wq_d.rearrange("(c p) m -> p c m", p=128))
        gp.op("dma_start", inc=16, out=wexp_buf[1][:, 0:KC, :],
              in_=wk_d.rearrange("(c p) m -> p c m", p=128))
        gp.op("dma_start", inc=16, out=wexp_buf[0][:, 8:12, :],
              in_=wv_d[0:512].rearrange("(c p) m -> p c m", p=128))
        gp.op("dma_start", inc=16, out=wexp_buf[1][:, 8:12, :],
              in_=wv_d[512:1024].rearrange("(c p) m -> p c m", p=128))
        gp.mark("att_wld")
        te.wait(te.at("gpsimd", "att_wld"))
        ve.op("memset", v_aug[:], 1.0)
        ve.mark("att_vaug")

        for t_i in range(NATT):
            tg = f"at{t_i}"
            for s in range(4):
                emit_ln(G, "aln", src, t_i * 4 + s, xn_a, s * 128, False, force_j=0)
            for w_i in range(3):
                for c in range(KC):
                    bq = 2 + (c % 2)
                    if w_i == 0 and c == 0:
                        te.wait(te.at("scalar", f"aln_evac_{t_i*4+3}"))
                    if w_i == 0 and c < 2:
                        if t_i > 0:
                            te.wait(te.at("scalar", f"at{t_i-1}_zev_2_{KC-2+c}"))
                    elif c < 2:
                        te.wait(te.at("scalar", f"{tg}_zev_{w_i-1}_{KC-2+c}"))
                    else:
                        te.wait(te.at("scalar", f"{tg}_zev_{w_i}_{c-2}"))
                    for k in range(KC):
                        if w_i < 2:
                            lhs = wexp_buf[w_i][:, k, c * 128:(c + 1) * 128]
                        else:
                            lhs = wexp_buf[k // 4][:, 8 + k % 4, c * 128:(c + 1) * 128]
                        te.op("matmul", bank(bq), lhs, xn_a[:, k, :],
                              inc=(1 if k == KC - 1 else 0), start=(k == 0), stop=(k == KC - 1))
                    te.mark(f"{tg}_mm_{w_i}_{c}")
                    sc.wait(sc.at("tensor", f"{tg}_mm_{w_i}_{c}"))
                    if w_i < 2:
                        if c >= 2:
                            sc.wait(sc.at("vector", f"{tg}_fma_{w_i}_{c-2}"))
                        elif w_i == 1:
                            sc.wait(sc.at("vector", f"{tg}_fma_0_{KC-2+c}"))
                        elif t_i > 0:
                            sc.wait(sc.at("vector", f"at{t_i-1}_fma_1_{KC-2+c}"))
                        sc.op("activation", z_ch[c % 2][:], bank(bq), AF.Identity,
                              bias=bias_t[:, w_i * KC + c: w_i * KC + c + 1])
                    else:
                        if c < 2 and t_i > 0:
                            sc.wait(sc.at("tensor", f"at{t_i-1}_vtr_3"))
                        sc.op("activation", vb_t[:, c, :], bank(bq), AF.Identity,
                              bias=bias_t[:, 2 * KC + c: 2 * KC + c + 1])
                    sc.mark(f"{tg}_zev_{w_i}_{c}")
                    if w_i < 2:
                        ve.wait(ve.at("scalar", f"{tg}_zev_{w_i}_{c}"))
                        if w_i == 0 and c < 2 and t_i > 0:
                            ve.wait(ve.at("sync", f"at{t_i-1}_qst"))
                            ve.wait(ve.at("tensor", f"at{t_i-1}_ktr_3"))
                        ve.op("tensor_scalar", r_ch[c % 2][:], z_ch[c % 2][:], 0.0, None, ALU.max)
                        ve.op("tensor_sub", mn_ch[c % 2][:], z_ch[c % 2][:], r_ch[c % 2][:])
                        ve.mark(f"{tg}_fma_{w_i}_{c}")
                        sc.wait(sc.at("vector", f"{tg}_fma_{w_i}_{c}"))
                        if c >= 2:
                            sc.wait(sc.at("vector", f"{tg}_fmc_{w_i}_{c-2}"))
                        elif w_i == 1:
                            sc.wait(sc.at("vector", f"{tg}_fmc_0_{KC-2+c}"))
                        elif t_i > 0:
                            sc.wait(sc.at("vector", f"at{t_i-1}_fmc_1_{KC-2+c}"))
                        sc.op("activation", mn_ch[c % 2][:], mn_ch[c % 2][:], AF.Exp)
                        sc.mark(f"{tg}_fmb_{w_i}_{c}")
                        ve.wait(ve.at("scalar", f"{tg}_fmb_{w_i}_{c}"))
                        dstt = qf_t if w_i == 0 else kb_t
                        ve.op("tensor_add", dstt[:, c, :], mn_ch[c % 2][:], r_ch[c % 2][:])
                        ve.mark(f"{tg}_fmc_{w_i}_{c}")
            sy.wait(sy.at("vector", f"{tg}_fmc_0_{KC-1}"))
            sy.op("dma_start", inc=16,
                  out=q_scr.rearrange("(c p) t -> p c t", p=128)[:, :, t_i * TAU:(t_i + 1) * TAU],
                  in_=qf_t[:])
            sy.mark(f"{tg}_qst")
            for s in range(4):
                te.wait(te.at("vector", f"{tg}_fmc_1_{KC-1}"))
                if not (t_i == 0 and s == 0):
                    te.wait(te.at("vector", f"{tg}_kev_{s-1}" if s > 0 else f"at{t_i-1}_kev_3"))
                for c in range(KC):
                    te.op("transpose", pbf[:, c * 128:(c + 1) * 128],
                          kb_t[:, c, s * 128:(s + 1) * 128], idb_t[:],
                          inc=(1 if c == KC - 1 else 0))
                te.mark(f"{tg}_ktr_{s}")
                ve.wait(ve.at("tensor", f"{tg}_ktr_{s}"))
                if not (t_i == 0 and s == 0):
                    ve.wait(ve.at("tensor", f"{tg}_kv_{s-1}" if s > 0 else f"at{t_i-1}_kv_3"))
                ve.op("tensor_copy", k_tm[:], pbf[:, 0:1024])
                ve.mark(f"{tg}_kev_{s}")
                te.wait(te.at("scalar", f"{tg}_zev_2_{KC-1}"))
                if not (t_i == 0 and s == 0):
                    te.wait(te.at("vector", f"{tg}_vev_{s-1}" if s > 0 else f"at{t_i-1}_vev_3"))
                for c in range(KC):
                    te.op("transpose", pbf[:, 1024 + c * 128: 1024 + (c + 1) * 128],
                          vb_t[:, c, s * 128:(s + 1) * 128], idb_t[:],
                          inc=(1 if c == KC - 1 else 0))
                te.mark(f"{tg}_vtr_{s}")
                ve.wait(ve.at("tensor", f"{tg}_vtr_{s}"))
                ve.op("tensor_copy",
                      v_aug[:].rearrange("p (h n) -> p h n", n=65)[:, :, 0:64],
                      pbf[:, 1024:2048].rearrange("p (h n) -> p h n", n=64))
                ve.mark(f"{tg}_vev_{s}")
                te.wait(te.at("vector", f"{tg}_kev_{s}"))
                te.wait(te.at("vector", f"{tg}_vev_{s}"))
                first = (t_i == 0 and s == 0)
                last = (t_i == NATT - 1 and s == 3)
                for h in range(NH):
                    b_i = 4 + h // 8
                    g_i = (h % 8) // 2
                    pr = (h % 2) * 64
                    te.op("matmul", bank(b_i, pr, 64, g_i * 65, 65),
                          k_tm[:, h * 64:(h + 1) * 64],
                          v_aug[:, h * 65:(h + 1) * 65],
                          inc=(1 if h == NH - 1 else 0),
                          start=(first and (h % 8) == 0),
                          stop=(last and (h % 8) == 7),
                          skip_group_check=True)
                te.mark(f"{tg}_kv_{s}")
        for e in ENGINES:
            G[e].mark("aa_end")

        ve.wait(ve.at("tensor", "aa_end"))
        ve.op("tensor_copy", kv_sb[:, 0:260], bank(4, 0, 128, 0, 260))
        ve.op("tensor_copy", kv_sb[:, 260:520], bank(5, 0, 128, 0, 260))
        ve.op("tensor_scalar", cc_sb[:, 0:520], kv_sb[:], bmask_t[:, 0:1], None, ALU.mult)
        ve.op("tensor_scalar", cc_sb[:, 520:1040], kv_sb[:], bmask_t[:, 1:2], None, ALU.mult)
        ve.mark("ab_ccsb")
        gp.wait(gp.at("vector", "ab_ccsb"))
        gp.op("dma_start", inc=16, out=cc_in.ap(), in_=cc_sb[:])
        gp.mark("ab_ccin")
        gp.wait(gp.at("gpsimd", "ab_ccin"))
        gp.op("collective_compute", "AllReduce", ALU.add,
              replica_groups=[[0, 1, 2, 3, 4, 5, 6, 7]],
              ins=[cc_in.ap()], outs=[cc_out.ap()], inc=1)
        gp.mark("ab_cc")
        gp.wait(gp.at("gpsimd", "ab_cc"))
        gp.op("dma_start", inc=16, out=cc_sb[:], in_=cc_out.ap())
        gp.wait(gp.at("tensor", "aa_end"))
        gp.op("dma_start", inc=16, out=wexp_buf[0][:, 0:KC, :],
              in_=wo_d.rearrange("(c p) m -> p c m", p=128))
        gp.mark("ab_rd")
        ve.wait(ve.at("gpsimd", "ab_rd"))
        ve.op("tensor_scalar", kv_my[:], cc_sb[:, 0:520], bmask_t[:, 0:1], None, ALU.mult)
        ve.op("scalar_tensor_tensor", kv_my[:], cc_sb[:, 520:1040], bmask_t[:, 1:2],
              kv_my[:], ALU.mult, ALU.add)
        ve.mark("ab_kvmy")
        sc.wait(sc.at("vector", "ab_kvmy"))
        sc.op("activation", kv_bf[:].rearrange("p a b -> p (a b)"), kv_my[:], AF.Copy)
        sc.mark("ab_kvbf")
        ve.wait(ve.at("scalar", "ab_kvbf"))
        ve.op("memset", den_l[:], 0.0)
        ve.op("memset", kvz[:].rearrange("p a b -> p (a b)"), 0.0)
        for h in range(NH):
            g_i = (h % 8) // 2
            ve.op("tensor_copy", den_l[(h % 2) * 64:(h % 2) * 64 + 64, h:h + 1],
                  kv_bf[(h % 2) * 64:(h % 2) * 64 + 64, h // 8, g_i * 65 + 64: g_i * 65 + 65])
            ve.op("tensor_copy",
                  kvz[(h % 2) * 64:(h % 2) * 64 + 64, h // 2, (h % 2) * 64:(h % 2) * 64 + 64],
                  kv_bf[(h % 2) * 64:(h % 2) * 64 + 64, h // 8, g_i * 65: g_i * 65 + 64])
        ve.mark("ab_denl")
        barrier(G, "ac")

        for t_i in range(NATT):
            tg = f"ac{t_i}"
            if t_i > 0:
                sc.wait(sc.at("tensor", f"ac{t_i-1}_omm_{KC-1}"))
            else:
                sc.wait(sc.at("vector", "ab_denl"))
            sc.op("dma_start", inc=16, out=qf_t[:],
                  in_=q_scr.rearrange("(c p) t -> p c t", p=128)[:, :, t_i * TAU:(t_i + 1) * TAU])
            sc.mark(f"{tg}_qld")
            te.wait(te.at("scalar", f"{tg}_qld"))
            for c in range(KC):
                ba = 2 + (c % 2)
                if not (t_i == 0 and c < 2):
                    te.wait(te.at("vector", f"{tg}_asc_{c-2}" if c >= 2 else f"ac{t_i-1}_asc_{KC-2+c}"))
                te.op("matmul", bank(ba), kvz[:, c, :], qf_t[:, c, :],
                      inc=1, start=True, stop=True)
                te.mark(f"{tg}_amm_{c}")
                if c == 0:
                    if t_i > 0:
                        te.wait(te.at("vector", f"ac{t_i-1}_radd_3"))
                else:
                    te.wait(te.at("vector", f"{tg}_zc_{c-1}"))
                te.op("matmul", bank(0, 0, 2, 0, TAU), den_l[:, 2 * c:2 * c + 2], qf_t[:, c, :],
                      inc=1, start=True, stop=True)
                te.mark(f"{tg}_dmm_{c}")
                sc.wait(sc.at("tensor", f"{tg}_dmm_{c}"))
                if c > 0:
                    sc.wait(sc.at("vector", f"{tg}_zc_{c-1}"))
                sc.op("activation", den_sb, bank(0, 0, 2, 0, TAU), AF.Identity, bias=cst_t[0:2, 1:2])
                sc.mark(f"{tg}_den_{c}")
                ve.wait(ve.at("scalar", f"{tg}_den_{c}"))
                ve.op("reciprocal", rc32, den_sb)
                ve.op("tensor_copy", zc_t, rc32)
                ve.mark(f"{tg}_zc_{c}")
                te.wait(te.at("vector", f"{tg}_zc_{c}"))
                if t_i == 0 and c == 0:
                    pass
                elif c == 0:
                    te.wait(te.at("vector", f"ac{t_i-1}_radd_3"))
                if not (t_i == 0 and c == 0):
                    te.wait(te.at("vector", f"{tg}_zbev_{c-1}" if c >= 1 else f"ac{t_i-1}_zbev_{KC-1}"))
                te.op("matmul", bank(1, 0, 128, 0, TAU), sel2_t[:], zc_t,
                      inc=1, start=True, stop=True)
                te.mark(f"{tg}_zb_{c}")
                ve.wait(ve.at("tensor", f"{tg}_zb_{c}"))
                ve.op("tensor_copy", zb_t[c % 2][:], bank(1, 0, 128, 0, TAU))
                ve.mark(f"{tg}_zbev_{c}")
                ve.wait(ve.at("tensor", f"{tg}_amm_{c}"))
                ve.op("tensor_mul", asc_t[:, c, :],
                      bankpair(1)[:, (c % 2) * 512:(c % 2) * 512 + 512], zb_t[c % 2][:])
                ve.mark(f"{tg}_asc_{c}")
            for c in range(KC):
                bo = 4 + (c % 2)
                if c == 0:
                    te.wait(te.at("vector", f"{tg}_asc_{KC-1}"))
                if not (t_i == 0 and c < 2):
                    te.wait(te.at("scalar", f"{tg}_oev_{c-2}" if c >= 2 else f"ac{t_i-1}_oev_{KC-2+c}"))
                for k in range(KC):
                    te.op("matmul", bank(bo), wexp_buf[0][:, k, c * 128:(c + 1) * 128],
                          asc_t[:, k, :],
                          inc=(1 if k == KC - 1 else 0), start=(k == 0), stop=(k == KC - 1))
                te.mark(f"{tg}_omm_{c}")
                sc.wait(sc.at("tensor", f"{tg}_omm_{c}"))
                if c < 2 and t_i > 0:
                    sc.wait(sc.at("tensor", f"ac{t_i-1}_rtr_3"))
                sc.op("activation", of_t[:, c, :], bank(bo), AF.Identity,
                      bias=bias_t[:, 3 * KC + c: 3 * KC + c + 1])
                sc.mark(f"{tg}_oev_{c}")
            for s in range(4):
                ug = t_i * 4 + s
                if s == 0:
                    te.wait(te.at("scalar", f"{tg}_oev_{KC-1}"))
                if not (t_i == 0 and s == 0):
                    te.wait(te.at("vector", f"{tg}_radd_{s-1}" if s > 0 else f"ac{t_i-1}_radd_3"))
                for c in range(KC):
                    te.op("transpose", bankpair(0)[:, c * 128:(c + 1) * 128],
                          of_t[:, c, s * 128:(s + 1) * 128], idf_t[:],
                          inc=(1 if c == KC - 1 else 0))
                te.mark(f"{tg}_rtr_{s}")
                if s >= 2:
                    sc.wait(sc.at("vector", f"{tg}_radd_{s-2}"))
                elif t_i > 0:
                    sc.wait(sc.at("vector", f"ac{t_i-1}_radd_{2+s}"))
                sc.op("dma_start", inc=16, out=xr_tm[s % 2][:], in_=src[ug * 128:(ug + 1) * 128, :])
                sc.mark(f"{tg}_rx_{s}")
                ve.wait(ve.at("tensor", f"{tg}_rtr_{s}"))
                ve.wait(ve.at("scalar", f"{tg}_rx_{s}"))
                if not (t_i == 0 and s < 2):
                    ve.wait(ve.at("sync", f"{tg}_rst_{s-2}" if s >= 2 else f"ac{t_i-1}_rst_{2+s}"))
                ve.op("tensor_add", o_tm[s % 2][:], bankpair(0), xr_tm[s % 2][:])
                ve.mark(f"{tg}_radd_{s}")
                sy.wait(sy.at("vector", f"{tg}_radd_{s}"))
                sy.op("dma_start", inc=16, out=dst[ug * 128:(ug + 1) * 128, :], in_=o_tm[s % 2][:])
                sy.mark(f"{tg}_rst_{s}")
        for e in ENGINES:
            G[e].mark("att_end")

    def schedule(G):
        sy, gp, sc, ve, te = (G[n] for n in ENGINES)
        sy.op("dma_start", inc=16, out=idf_t[:], in_=idf_d)
        sy.op("dma_start", inc=16, out=idb_t[:], in_=idb_d)
        sy.op("dma_start", inc=16, out=id32_t[:], in_=id32_d)
        sy.op("dma_start", inc=16, out=sel2_t[:], in_=sel2_d)
        sy.op("dma_start", inc=16,
              out=wrt_t[:],
              in_=wrt_d.rearrange("(c p) e -> p c e", p=128))
        sy.op("dma_start", inc=16, out=ebias_t[:], in_=ebias_d)
        sy.op("dma_start", inc=16, out=bias_t[:], in_=bias_d)
        sy.op("dma_start", inc=16, out=bmask_t[:], in_=bmask_d)
        sy.op("dma_start", inc=16, out=cst_t[:], in_=cst_d)
        sy.mark("prologue_dma")
        for e in ["vector", "scalar", "tensor", "gpsimd"]:
            G[e].wait(G[e].at("sync", "prologue_dma"))
        ve.op("memset", stat[:], 0.0)
        ve.op("memset", comb[32:34, :], 0.5)
        ve.mark("prologue")
        for _ in range(4):
            gp.op("nop")
        gp.mark("prologue_gp")

        if stage in ("ln1", "ln2"):
            cut = int(os.environ.get("KMK_CUT", "99"))
            tag = "f0s0"
            barrier(G, f"{tag}_a")
            for u in range(2):
                emit_ln(G, "f0ln", xin, u, xn_fm, u * 128, stage == "ln2", cut=cut)
                if u == 0:
                    ve.op("tensor_copy", uw_t[0][:, 0:16], stat[:])
                    ve.mark("statsnap")
            sy.wait(sy.at("vector", "statsnap"))
            sy.op("dma_start", inc=16, out=out_d[256:384, 0:16], in_=uw_t[0][:, 0:16])
            ve.wait(ve.at("scalar", "f0ln_evac_1"))
            ve.op("tensor_copy", wbc_t[:, 0:1024].rearrange("p (c t) -> p c t", t=128), xn_fm[:, :, 0:128])
            ve.mark("dump1")
            sy.wait(sy.at("vector", "dump1"))
            sy.op("dma_start", inc=16, out=out_d[0:128, :].rearrange("p (c t) -> p c t", t=128), in_=wbc_t[:, 0:1024].rearrange("p (c t) -> p c t", t=128))
            if stage == "ln2" and int(os.environ.get("KMK_CUT", "99")) >= 99:
                sy.wait(sy.at("vector", "f0ln_combem_1"))
                sy.op("dma_start", inc=16, out=out_d[128:162, 0:256], in_=comb[:, 0:256])
        elif stage == "fa" or stage == "fb1":
            tag = "f0s0"
            barrier(G, f"{tag}_a")
            for u in range(NU):
                emit_ln(G, "f0ln", xin, u, xn_fm, u * 128, True)
            barrier(G, f"{tag}_b")
            if stage == "fb1":
                emit_experts(G, tag, nloop=1)
                barrier(G, f"{tag}_c")
                sy.wait(sy.at("vector", f"{tag}_sweep_end"))
                sy.op("dma_start", inc=16,
                      out=out_d[0:2048, :].rearrange("(r p) m -> p r m", p=128),
                      in_=dacc[:].rearrange("p c t -> p (c t)").rearrange(
                          "p (r m) -> p r m", m=1024))
            else:
                sy.wait(sy.at("vector", f"f0ln_combem_{NU-1}"))
                sy.op("dma_start", inc=16, out=out_d[0:34, 0:1024], in_=comb[:, 0:1024])
                sy.op("dma_start", inc=16, out=out_d[34:68, 0:1024], in_=comb[:, 1024:2048])
        elif stage == "f1":
            emit_f_call(G, 0, xin, out_d)
        else:
            emit_f_call(G, 0, xin, x2_scr)
            if stage == "a":
                emit_attn(G, x2_scr, out_d)
            elif stage == "adbg":
                emit_attn(G, x2_scr, x3_scr)
                ve.wait(ve.at("vector", "att_end"))
                ve.op("tensor_copy", wbc_t[:, 0:520], kv_my[:, :])
                ve.mark("dbg1")
                sy.wait(sy.at("vector", "dbg1"))
                sy.op("dma_start", inc=16, out=out_d[0:128, 0:520], in_=wbc_t[:, 0:520])
                sy.op("dma_start", inc=16, out=qf_t[:],
                      in_=q_scr.rearrange("(c p) t -> p c t", p=128)[:, :, 0:TAU])
                sy.mark("dbgq")
                ve.wait(ve.at("sync", "dbgq"))
                ve.op("tensor_copy", wbc_t[:].rearrange("p (c t) -> p c t", t=TAU),
                      qf_t[:, 0:4, :])
                ve.mark("dbg2")
                sy.wait(sy.at("vector", "dbg2"))
                sy.op("dma_start", inc=16,
                      out=out_d[1024:1280, :].rearrange("(c2 p) m -> p c2 m", p=128),
                      in_=wbc_t[:].rearrange("p (c2 m) -> p c2 m", m=1024))
                sy.mark("dbg2x")
                ve.wait(ve.at("vector", "dbg2"))
                ve.wait(ve.at("sync", "dbg2x"))
                ve.op("tensor_copy", o_tm[0][:, 0:16], den_l[:, :])
                ve.mark("dbg3")
                sy.wait(sy.at("vector", "dbg3"))
                sy.op("dma_start", inc=16, out=out_d[256:384, 0:16], in_=o_tm[0][:, 0:16])
                ve.op("tensor_copy", o_tm[1][:].rearrange("p (c t) -> p c t", t=TAU)[:, 0:2, :],
                      asc_t[:, 0:2, :])
                ve.mark("dbg4")
                sy.wait(sy.at("vector", "dbg4"))
                sy.op("dma_start", inc=16, out=out_d[384:512, :], in_=o_tm[1][:, :])
                ve.op("tensor_copy", o_tm[0][:, 16:18].rearrange("p a -> a p") if False else o_tm[0][0:2, 16:528], den_sb)
                ve.mark("dbg5")
                sy.wait(sy.at("vector", "dbg5"))
                sy.op("dma_start", inc=16, out=out_d[1280:1282, 0:512], in_=o_tm[0][0:2, 16:528])
                sy.op("dma_start", inc=16, out=out_d[1284:1412, 0:512], in_=zb_t[1][:, :])
                sy.op("dma_start", inc=16, out=out_d[512:1024, :], in_=x3_scr[0:512, :])
            else:
                emit_attn(G, x2_scr, x3_scr)
                emit_f_call(G, 1, x3_scr, out_d)

        for e in ENGINES:
            G[e].mark("the_end")
        if sy.emit:
            sy.eng.wait_ge(sems["sync"], plan[("sync", "the_end")].a)

    G0 = {n: Eng(n, plan, False, record=True) for n in ENGINES}
    schedule(G0)
    totals = {n: G0[n].cnt for n in ENGINES}

    with nc.Block() as block:
        def mk_section(name):
            def section(eng):
                G = {n: Eng(n, plan, n == name, eng if n == name else None, sems)
                     for n in ENGINES}
                schedule(G)
                assert G[name].cnt == totals[name], f"{name}: {G[name].cnt} != {totals[name]}"
            return section

        block.sync(mk_section("sync"))
        block.gpsimd(mk_section("gpsimd"))
        block.scalar(mk_section("scalar"))
        block.vector(mk_section("vector"))
        block.tensor(mk_section("tensor"))

    return nc, totals


def _prep_inputs(inp):
    import ml_dtypes
    f = np.float32
    gupT = np.transpose(np.asarray(inp["L0_ffn_gate_up_proj"], f), (0, 2, 1))
    sg = np.asarray(inp["L0_ffn_shared_gate_w"], f).T
    su = np.asarray(inp["L0_ffn_shared_up_w"], f).T
    sd = np.asarray(inp["L0_ffn_shared_down_w"], f).T
    ps1 = np.concatenate([sg[:, 0:512], su[:, 0:512]], axis=1)[None]
    ps2 = np.concatenate([sg[:, 512:1024], su[:, 512:1024]], axis=1)[None]
    gup_all = np.ascontiguousarray(np.concatenate([gupT, ps1, ps2], axis=0))
    dwnT = np.transpose(np.asarray(inp["L0_ffn_down_proj"], f), (0, 2, 1))
    dwn_all = np.ascontiguousarray(
        np.concatenate([dwnT, sd[None, 0:512, :], sd[None, 512:1024, :]], axis=0))

    def lhsT(w):
        return np.ascontiguousarray(np.asarray(w, f).T)

    def bias_col(b):
        return np.ascontiguousarray(np.asarray(b, f).reshape(KC, 128).T)

    biases = np.concatenate([bias_col(inp["L0_attn_bq"]), bias_col(inp["L0_attn_bk"]),
                             bias_col(inp["L0_attn_bv"]), bias_col(inp["L0_attn_bo"])],
                            axis=1)

    wexp = np.ascontiguousarray(np.concatenate([gup_all, dwn_all], axis=1))
    common = {
        "wexp": wexp,
        "wq": lhsT(inp["L0_attn_Wq"]), "wk": lhsT(inp["L0_attn_Wk"]),
        "wv": lhsT(inp["L0_attn_Wv"]), "wo": lhsT(inp["L0_attn_Wo"]),
        "wrt": lhsT(inp["L0_ffn_router_weight"]),
        "bqkvo": np.ascontiguousarray(biases),
        "ebias": np.ascontiguousarray(
            np.broadcast_to(np.asarray(inp["L0_ffn_e_score_bias"], f), (128, 32))),
        "idf": np.eye(128, dtype=f),
        "idb": np.eye(128).astype(ml_dtypes.bfloat16),
        "id32": np.eye(32, dtype=f),
        "sel2": np.repeat(np.eye(2, dtype=f), 64, axis=1),
        "cst": np.ascontiguousarray(np.broadcast_to(
            np.array([EPS, 1e-6, -0.5, 0.0], f), (128, 4))),
    }
    x = np.asarray(inp["L0_attn_hidden_states"], dtype=np.float32)
    in_maps = []
    for c in range(8):
        b, s0 = c // 4, (c % 4) * TC
        m = dict(common)
        m["xin"] = np.ascontiguousarray(x[b, s0:s0 + TC, :])
        bm = np.zeros((128, 2), f)
        bm[:, b] = 1.0
        m["bmask"] = bm
        in_maps.append(m)
    return in_maps


_CACHE = {}


def kernel(**inputs):
    stage = os.environ.get("KMK_STAGE", "full")
    in_maps = _prep_inputs(inputs)
    if stage not in _CACHE:
        _CACHE[stage] = build_program(stage)
    nc, _ = _CACHE[stage]
    res = run_bass_kernel_spmd(nc, in_maps, core_ids=list(range(8)))
    full = np.empty((2, 16384, D), np.float32)
    for c in range(8):
        full[c // 4, (c % 4) * TC:(c % 4 + 1) * TC, :] = res.results[c]["out"]
    return full
